# version 14 (final)
# baseline (speedup 1.0000x reference)
import numpy as np
import ml_dtypes
from contextlib import ExitStack

import concourse.bass as bass
import concourse.mybir as mybir
from concourse.bass_utils import run_bass_kernel_spmd

F32 = mybir.dt.float32
BF16 = mybir.dt.bfloat16
I32 = mybir.dt.int32
AF = mybir.ActivationFunctionType
ALU = mybir.AluOpType
AX = mybir.AxisListType

B, T, D = 4, 4096, 2048
H, DR = 16, 64
CQ, CKV = 1536, 512
OWN = 2048
NCH = D // 128
NCQ = CQ // 128
NKV = CKV // 128
G = 512
NSLOT = OWN // G
NKT = [8, 16, 24, 32]
SCALE = float(1.0 / np.sqrt(192.0))
EPS1, EPS2, EPSR = 1e-5, 1e-5, 1e-6
BF = ml_dtypes.bfloat16
BLOCKS = {0: [0, 3, 5, 6], 1: [1, 2, 4, 7]}
OP3 = [list(range(4 * i, 4 * i + 4)) for i in range(4)]
LASTP = 3

ENGINES = ("sync", "gpsimd", "scalar", "vector", "tensor")


def _prev_chunk_user(k):
    return k + 12 if k + 12 <= 15 else k + 6



class Prog:
    def __init__(self, nc, ctx):
        self.nc = nc
        self.ctx = ctx
        self.ops = {e: [] for e in ENGINES}
        self.sem = {}
        self.cnt = {}
        for name in ("qS", "qG", "qA", "pe", "ve", "ac"):
            self.sem[name] = ctx.enter_context(nc.semaphore(name))
            self.cnt[name] = 0

    def new_sem(self, name):
        if name not in self.sem:
            self.sem[name] = self.ctx.enter_context(self.nc.semaphore(name))
            self.cnt[name] = 0

    def op(self, eng, fn, waits=(), inc=None, inc_k=1):
        sem = self.sem
        w = [(sem[s], v) for (s, v) in waits if v and v > 0]
        if inc is not None:
            self.cnt[inc] += inc_k
            tgt = (sem[inc], inc_k)
        else:
            tgt = None

        def f(eng_obj, w=w, tgt=tgt, fn=fn):
            for s, v in w:
                eng_obj.wait_ge(s, v)
            ins = fn(eng_obj)
            if tgt is not None:
                if ins is None:
                    ins = eng_obj.nop()
                ins.then_inc(*tgt)
        self.ops[eng].append(f)
        return self.cnt[inc] if inc is not None else None

    def dma(self, queue, out_ap, in_ap, waits=(), sem=None):
        if sem is None:
            s = {"sync": "qS", "scalar": "qA", "gpsimd": "qG"}[queue]
            waits = list(waits) + [(s, self.cnt[s])]
        else:
            s = sem
            self.new_sem(s)
        return self.op(queue, lambda e: e.dma_start(out=out_ap, in_=in_ap),
                       waits=waits, inc=s, inc_k=16)

    def barrier(self):
        snap = [(s, v) for s, v in self.cnt.items() if v > 0]
        for e in ENGINES:
            self.op(e, lambda eng: None, waits=snap)


def build_program():
    nc = bass.Bass("TRN2", target_bir_lowering=False)
    dt_ = nc.dram_tensor

    x0T = dt_("x0T", [D, T], F32, kind="ExternalInput").ap()
    wfT = dt_("wfT", [D, D], F32, kind="ExternalInput").ap()
    wqa = dt_("wqa", [D, CQ], F32, kind="ExternalInput").ap()
    wqbn = dt_("wqbn", [CQ, H * 128], F32, kind="ExternalInput").ap()
    wqbr = dt_("wqbr", [CQ, H * DR], F32, kind="ExternalInput").ap()
    wkva = dt_("wkva", [D, CKV], F32, kind="ExternalInput").ap()
    wkvb = dt_("wkvb", [CKV, H * 256], F32, kind="ExternalInput").ap()
    wkr = dt_("wkr", [D, DR], F32, kind="ExternalInput").ap()
    wo = dt_("wo", [H * 128, D], F32, kind="ExternalInput").ap()
    gamma_t = dt_("gamma_t", [128, NCH], F32, kind="ExternalInput").ap()
    beta_t = dt_("beta_t", [128, NCH], F32, kind="ExternalInput").ap()
    lnw_t = dt_("lnw_t", [128, NCH], F32, kind="ExternalInput").ap()
    lnb_t = dt_("lnb_t", [128, NCH], F32, kind="ExternalInput").ap()
    qnw_t = dt_("qnw_t", [128, NCQ], F32, kind="ExternalInput").ap()
    kvnw_t = dt_("kvnw_t", [128, NKV], F32, kind="ExternalInput").ap()
    ones_cb = dt_("ones_cb", [128, 1], BF16, kind="ExternalInput").ap()
    ones_rf = dt_("ones_rf", [1, 128], F32, kind="ExternalInput").ap()
    ident_f = dt_("ident_f", [128, 128], F32, kind="ExternalInput").ap()
    mskin = dt_("mskin", [128, 32 * G], BF16, kind="ExternalInput").ap()
    cs_q = dt_("cs_q", [64, OWN], BF16, kind="ExternalInput").ap()
    sc_q = dt_("sc_q", [64, OWN], BF16, kind="ExternalInput").ap()
    cs_k = dt_("cs_k", [64, T], BF16, kind="ExternalInput").ap()
    sc_k = dt_("sc_k", [64, T], BF16, kind="ExternalInput").ap()
    offs = dt_("offs", [1, 8], I32, kind="ExternalInput").ap()

    yT = dt_("yT", [128, NCH * OWN], F32, kind="ExternalOutput").ap()

    d_sw = dt_("d_sw", [128, NCH * D], BF16).ap()
    d_x1 = dt_("d_x1", [128, NCH * T], F32).ap()
    d_x1n = dt_("d_x1n", [128, NCH * T], BF16).ap()
    d_cqn = dt_("d_cqn", [128, NCQ * OWN], BF16).ap()
    d_qn = dt_("d_qn", [128, H * OWN], BF16).ap()
    d_qr = dt_("d_qr", [64, H * OWN], BF16).ap()
    d_ot = dt_("d_ot", [128, H * OWN], BF16).ap()
    d_x2 = dt_("d_x2", [128, NCH * OWN], F32).ap()
    d_zn2 = dt_("d_zn2", [128, NCH * OWN], BF16).ap()

    def v3(ap2d, nch):
        return ap2d.rearrange("p (c t) -> p c t", c=nch)

    with ExitStack() as X:
        E = X.enter_context
        p = Prog(nc, X)
        ps = [E(nc.psum_tensor(f"ps{i}", [128, 512], F32)) for i in range(8)]

        c_onesb = E(nc.sbuf_tensor("c_onesb", [128, 1], BF16))
        c_onesrf = E(nc.sbuf_tensor("c_onesrf", [1, 128], F32))
        c_gam = E(nc.sbuf_tensor("c_gam", [128, NCH], F32))
        c_bet = E(nc.sbuf_tensor("c_bet", [128, NCH], F32))
        c_lnw = E(nc.sbuf_tensor("c_lnw", [128, NCH], F32))
        c_lnb = E(nc.sbuf_tensor("c_lnb", [128, NCH], F32))
        c_qnw = E(nc.sbuf_tensor("c_qnw", [128, NCQ], F32))
        c_kvnw = E(nc.sbuf_tensor("c_kvnw", [128, NKV], F32))
        c_Gam = E(nc.sbuf_tensor("c_Gam", [128, NCH], F32))
        c_B2 = E(nc.sbuf_tensor("c_B2", [128, NCH], F32))
        c_scol = E(nc.sbuf_tensor("c_scol", [128, 1], F32))
        c_csq = E(nc.sbuf_tensor("c_csq", [64, OWN], BF16))
        c_scq = E(nc.sbuf_tensor("c_scq", [64, OWN], BF16))
        c_csk = E(nc.sbuf_tensor("c_csk", [64, T], BF16))
        c_sck = E(nc.sbuf_tensor("c_sck", [64, T], BF16))
        c_offs = E(nc.sbuf_tensor("c_offs", [1, 8], I32))
        c_idb = E(nc.sbuf_tensor("c_idb", [128, 128], BF16))
        ckvT = E(nc.sbuf_tensor("ckvT", [128, NKV * T], BF16))
        krotT = E(nc.sbuf_tensor("krotT", [64, T], BF16))
        qmax = E(nc.sbuf_tensor("qmax", [1, 128], F32))
        kstat = E(nc.sbuf_tensor("kstat", [1, 64], F32))
        rows = E(nc.sbuf_tensor("rows", [1, 5 * G], F32))

        def rowchain(pe_end, bank_sum, bank_sq, double, denom, extra_w=()):
            p.op("vector",
                 lambda e: e.tensor_scalar(rows[0:1, 0:G],
                                           bank_sum[0:1, 0:G],
                                           1.0 / denom, None, ALU.mult),
                 waits=[("pe", pe_end)] + list(extra_w))
            v0 = p.op("vector",
                      lambda e: e.tensor_scalar(rows[0:1, G:2 * G],
                                                bank_sq[0:1, 0:G],
                                                1.0 / denom, None, ALU.mult),
                      inc="ve")
            p.op("vector",
                 lambda e: e.tensor_mul(rows[0:1, 4 * G:5 * G],
                                        rows[0:1, 0:G], rows[0:1, 0:G]))
            p.op("vector",
                 lambda e: e.tensor_sub(rows[0:1, G:2 * G],
                                        rows[0:1, G:2 * G],
                                        rows[0:1, 4 * G:5 * G]))
            if double:
                v1 = p.op("vector",
                          lambda e: e.tensor_scalar(
                              rows[0:1, 2 * G:3 * G], rows[0:1, G:2 * G],
                              1.0 + EPS2, EPS1 * EPS2, ALU.mult, ALU.add),
                          inc="ve")
            else:
                v1 = p.op("vector",
                          lambda e: e.tensor_scalar(
                              rows[0:1, 2 * G:3 * G], rows[0:1, G:2 * G],
                              1.0, EPS1, ALU.mult, ALU.add), inc="ve")
            p.op("scalar",
                 lambda e: e.activation(rows[0:1, 2 * G:3 * G],
                                        rows[0:1, 2 * G:3 * G], AF.Ln),
                 waits=[("ve", v1)])
            a1c = p.op("scalar",
                       lambda e: e.activation(rows[0:1, 2 * G:3 * G],
                                              rows[0:1, 2 * G:3 * G],
                                              AF.Exp, scale=-0.5), inc="ac")
            vmr = p.op("vector",
                       lambda e: e.tensor_mul(rows[0:1, 3 * G:4 * G],
                                              rows[0:1, 0:G],
                                              rows[0:1, 2 * G:3 * G]),
                       waits=[("ac", a1c)], inc="ve")
            return a1c, vmr, v0

        def chunk_stats(src_f32, castbuf, sqbuf, bank_sum, bank_sq, first_w):
            pe_sum = {}
            pe_sq = {-1: 0}
            ac_cast = {}
            ve_sq = {-1: 0}
            for c in range(NCH):
                ac_cast[c] = p.op(
                    "scalar",
                    lambda e, c=c: e.copy(castbuf[:, :],
                                          src_f32[:, c * G:(c + 1) * G]),
                    waits=(list(first_w) if c == 0 else
                           [("pe", pe_sum[c - 1]), ("ve", ve_sq[c - 1])]),
                    inc="ac")
                ve_sq[c] = p.op(
                    "vector",
                    lambda e: e.tensor_mul(sqbuf[:, :], castbuf[:, :],
                                           castbuf[:, :]),
                    waits=[("ac", ac_cast[c]), ("pe", pe_sq[c - 1])],
                    inc="ve")
                pe_sum[c] = p.op(
                    "tensor",
                    lambda e, c=c: e.matmul(bank_sum[0:1, 0:G],
                                            c_onesb[:, 0:1], castbuf[:, :],
                                            start=(c == 0),
                                            stop=(c == NCH - 1)),
                    waits=[("ac", ac_cast[c])], inc="pe")
                pe_sq[c] = p.op(
                    "tensor",
                    lambda e, c=c: e.matmul(bank_sq[0:1, 0:G],
                                            c_onesb[:, 0:1], sqbuf[:, :],
                                            start=(c == 0),
                                            stop=(c == NCH - 1)),
                    waits=[("ve", ve_sq[c])], inc="pe")
            return pe_sq[NCH - 1], ac_cast[NCH - 1], ve_sq[NCH - 1]

        def chunk_stats_b(src_b16, sqbuf, bank_sum, bank_sq, first_w,
                          nch=NCH, rows64=False):
            onesc = c_onesb[0:64, 0:1] if rows64 else c_onesb[:, 0:1]
            pe_sq = {-1: 0}
            ve_sq = {}
            for c in range(nch):
                p.op("tensor",
                     lambda e, c=c: e.matmul(
                         bank_sum[0:1, 0:G], onesc,
                         src_b16[:, c * G:(c + 1) * G],
                         start=(c == 0), stop=(c == nch - 1)),
                     waits=(list(first_w) if c == 0 else ()))
            for c in range(nch):
                ve_sq[c] = p.op(
                    "vector",
                    lambda e, c=c: e.tensor_mul(
                        sqbuf[:, :], src_b16[:, c * G:(c + 1) * G],
                        src_b16[:, c * G:(c + 1) * G]),
                    waits=([("pe", pe_sq[c - 1])] if c > 0 else
                           list(first_w)), inc="ve")
                pe_sq[c] = p.op(
                    "tensor",
                    lambda e, c=c: e.matmul(
                        bank_sq[0:1, 0:G], onesc, sqbuf[:, :],
                        start=(c == 0), stop=(c == nch - 1)),
                    waits=[("ve", ve_sq[c])], inc="pe")
            return pe_sq[nch - 1], ve_sq[nch - 1]

        p.dma("sync", c_onesb[:, :], ones_cb)
        p.dma("sync", c_onesrf[:, :], ones_rf)
        p.dma("sync", c_gam[:, :], gamma_t)
        p.dma("sync", c_bet[:, :], beta_t)
        p.dma("sync", c_lnw[:, :], lnw_t)
        p.dma("sync", c_lnb[:, :], lnb_t)
        p.dma("sync", c_qnw[:, :], qnw_t)
        p.dma("sync", c_kvnw[:, :], kvnw_t)
        p.dma("sync", c_csq[:, :], cs_q)
        p.dma("sync", c_scq[:, :], sc_q)
        p.dma("sync", c_csk[:, :], cs_k)
        p.dma("sync", c_sck[:, :], sc_k)
        cl = p.dma("sync", c_offs[:, :], offs)
        idbl = p.dma("gpsimd", c_idb[:, :], ident_f)

        with nc.sbuf_tensor("a0_st", [128, 2 * D], F32) as stage, \
             nc.sbuf_tensor("a0_sw", [128, 2 * D], BF16) as swt, \
             nc.sbuf_tensor("a0_wm", [128, NCH], F32) as wmax, \
             nc.sbuf_tensor("a0_id", [128, 128], F32) as identt, \
             nc.sbuf_tensor("a0_wc", [128, 1], F32) as wcol:
            idl = p.dma("sync", identt[:, :], ident_f)
            st_c, ac_c, ve_c, sp_c = {}, {}, {}, {}
            for k in range(NCH):
                b = k % 2
                st_c[k] = p.dma("sync", stage[:, b * D:(b + 1) * D],
                                wfT[k * 128:(k + 1) * 128, :],
                                waits=[("ac", ac_c.get(k - 2, 0)),
                                       ("ve", ve_c.get(k - 2, 0))])
                ac_c[k] = p.op("scalar",
                               lambda e, b=b: e.sign(
                                   swt[:, b * D:(b + 1) * D],
                                   stage[:, b * D:(b + 1) * D]),
                               waits=[("qS", st_c[k]),
                                      (f"qSW{k % 2}", sp_c.get(k - 2, 0))],
                               inc="ac")
                ve_c[k] = p.op("vector",
                               lambda e, b=b, k=k: e.reduce_max(
                                   wmax[:, k:k + 1],
                                   stage[:, b * D:(b + 1) * D],
                                   axis=AX.X, apply_absolute_value=True),
                               waits=[("qS", st_c[k])], inc="ve")
                sp_c[k] = p.dma("gpsimd", v3(d_sw, NCH)[:, k, :],
                                swt[:, b * D:(b + 1) * D],
                                waits=[("ve", ve_c[k]), ("ac", ac_c[k])],
                                sem=f"qSW{k % 2}")
            vm = p.op("vector",
                      lambda e: e.reduce_max(wcol[:, 0:1], wmax[:, :],
                                             axis=AX.X),
                      waits=[("ve", ve_c[NCH - 1])], inc="ve")
            tm = p.op("tensor",
                      lambda e: e.transpose(ps[0][0:1, 0:128], wcol[:, 0:1],
                                            identt[:, :]),
                      waits=[("ve", vm), ("qS", idl)], inc="pe")
            vs = p.op("vector",
                      lambda e: e.reduce_max(rows[0:1, 0:1],
                                             ps[0][0:1, 0:128], axis=AX.X),
                      waits=[("pe", tm)], inc="ve")
            tb = p.op("tensor",
                      lambda e: e.matmul(ps[1][:, 0:1], c_onesrf[0:1, :],
                                         rows[0:1, 0:1], start=True,
                                         stop=True),
                      waits=[("ve", vs)], inc="pe")
            vs = p.op("vector",
                      lambda e: e.tensor_copy(c_scol[:, 0:1], ps[1][:, 0:1]),
                      waits=[("pe", tb)], inc="ve")
            p.op("vector",
                 lambda e: e.tensor_scalar(c_Gam[:, :], c_gam[:, :],
                                           c_scol[:, 0:1], 0.5, ALU.mult,
                                           ALU.mult),
                 waits=[("ve", vs)])
            a0_ve = p.op("vector",
                         lambda e: e.tensor_scalar(c_B2[:, :], c_bet[:, :],
                                                   0.5, None, ALU.mult),
                         inc="ve")
            a0_sw0 = p.cnt["qSW0"]
            a0_sw1 = p.cnt["qSW1"]
            a0_pe = p.cnt["pe"]

        p.barrier()

        NG1 = T // G
        with nc.sbuf_tensor("a1_swb", [128, 6 * 512], BF16) as swb, \
             nc.sbuf_tensor("a1_x0f", [128, NCH * G], F32) as x0f, \
             nc.sbuf_tensor("a1_x0b", [128, 2 * NCH * G], BF16) as x0b, \
             nc.sbuf_tensor("a1_sq", [128, G], BF16) as sqb, \
             nc.sbuf_tensor("a1_cb", [128, G], BF16) as castb, \
             nc.sbuf_tensor("a1_zn", [128, 2 * NCH * G], BF16) as zn, \
             nc.sbuf_tensor("a1_zn1", [128, NCH * G], BF16) as zn1, \
             nc.sbuf_tensor("a1_bc", [128, 4 * G], F32) as bc, \
             nc.sbuf_tensor("a1_bcB", [128, 2 * G], BF16) as bcB:
            x0v = x0T.rearrange("(c p) t -> p c t", p=128)
            WZ = NCH * G
            swv = v3(d_sw, NCH)

            ld_f, ld_b, ld_sw = {}, {}, {}
            ve_zn, ve_zn1 = {}, {}

            def emit_ldb(g, pe_stm, stm_ve):
                if g >= NG1:
                    return
                xb_ = x0b[:, (g % 2) * NCH * G:(g % 2 + 1) * NCH * G]
                ld_b[g] = p.dma(
                    "gpsimd", v3(xb_, NCH), x0v[:, :, g * G:(g + 1) * G],
                    waits=[("pe", pe_stm.get(g - 2, 0)),
                           ("ve", stm_ve.get(g - 2, 0))], sem=f"qB{g % 2}")
            ve_epi = {}
            ve_v0m, ve_v0t = {}, {}
            ac_tail = {}
            pe_stm, pe_stt, pe_bcm, pe_bct, pe_pass, pe_swc = {}, {}, {}, {}, {}, {}
            sp_x1, sp_zn1 = {}, {}
            vbm, vbt = {}, {}
            stm_ve, stt_ac, stt_ve = {}, {}, {}

            for g in range(NG1):
                ld_f[g] = p.dma(
                    "sync", v3(x0f[:, :], NCH), x0v[:, :, g * G:(g + 1) * G],
                    waits=[("qX", sp_x1.get(g - 1, 0)),
                           ("ac", ac_tail.get(g - 1, 0)),
                           ("ve", ve_zn1.get(g - 1, 0))])
                if g == 0:
                    emit_ldb(0, pe_stm, stm_ve)
                    emit_ldb(1, pe_stm, stm_ve)
                xb = x0b[:, (g % 2) * NCH * G:(g % 2 + 1) * NCH * G]
                pe_stm[g], stm_ve[g] = chunk_stats_b(
                    xb, sqb, ps[4], ps[5],
                    [(f"qB{g % 2}", ld_b[g]), ("ve", ve_v0t.get(g - 1, 0)),
                     ("pe", pe_stt.get(g - 1, 0)),
                     ("pe", a0_pe if g == 0 else 0)])
                emit_ldb(g + 2, pe_stm, stm_ve)
                ac_rm, ve_mrm, ve_v0m[g] = rowchain(
                    pe_stm[g], ps[4], ps[5], True, D,
                    extra_w=[("pe", pe_bct.get(g - 1, 0))])
                p.op("tensor",
                     lambda e: e.matmul(ps[6][:, 0:G], c_onesrf[0:1, :],
                                        rows[0:1, 2 * G:3 * G],
                                        start=True, stop=True),
                     waits=[("ac", ac_rm), ("ve", ve_mrm),
                            ("ve", vbt.get(g - 1, 0))])
                pe_bcm[g] = p.op(
                    "tensor",
                    lambda e: e.matmul(ps[7][:, 0:G], c_onesrf[0:1, :],
                                       rows[0:1, 3 * G:4 * G],
                                       start=True, stop=True), inc="pe")
                p.op("vector",
                     lambda e: e.tensor_copy(bc[:, 0:G], ps[6][:, 0:G]),
                     waits=[("pe", pe_bcm[g])])
                vbm[g] = p.op("vector",
                              lambda e: e.tensor_copy(bc[:, G:2 * G],
                                                      ps[7][:, 0:G]),
                              inc="ve")
                p.op("vector",
                     lambda e: e.tensor_copy(bcB[:, 0:G], bc[:, G:2 * G]))
                zb = (g % 2) * WZ
                p.op("vector",
                     lambda e, zb=zb: e.tensor_tensor(
                         v3(zn[:, zb:zb + WZ], NCH), v3(x0f[:, :], NCH),
                         bc[:, 0:G].unsqueeze(1).broadcast_to((128, NCH, G)),
                         ALU.mult),
                     waits=[("qS", ld_f[g]),
                            ("pe", pe_pass.get((g - 2, LASTP), 0))])
                p.op("vector",
                     lambda e, zb=zb: e.tensor_tensor(
                         v3(zn[:, zb:zb + WZ], NCH),
                         v3(zn[:, zb:zb + WZ], NCH),
                         bcB[:, 0:G].unsqueeze(1).broadcast_to((128, NCH, G)),
                         ALU.subtract))
                for c in range(NCH):
                    ve_zn[g] = p.op(
                        "vector",
                        lambda e, c=c, zb=zb: e.tensor_scalar(
                            zn[:, zb + c * G: zb + (c + 1) * G],
                            zn[:, zb + c * G: zb + (c + 1) * G],
                            c_lnw[:, c:c + 1], c_lnb[:, c:c + 1],
                            ALU.mult, ALU.add),
                        inc="ve" if c == NCH - 1 else None)
                zb = (g % 2) * WZ
                for pi, olist in enumerate(OP3):
                    o0 = olist[0]
                    for k in range(NCH):
                        pk = _prev_chunk_user(k) if k < 6 else None
                        if pk is not None:
                            wv = pe_swc.get((g, pi - 1, pk),
                                            pe_swc.get((g - 1, LASTP, pk), 0))
                        else:
                            wv = pe_swc[(g, pi, k - 6)]
                        ld_sw[(g, pi, k)] = p.dma(
                            "scalar",
                            swb[:, (k % 6) * 512:
                                (k % 6) * 512 + len(olist) * 128],
                            swv[:, k, o0 * 128:(o0 + len(olist)) * 128],
                            waits=[("pe", wv)] +
                                  ([("qSW0", a0_sw0), ("qSW1", a0_sw1)]
                                   if (g == 0 and pi == 0) else []),
                            sem=f"qW{k % 6}")
                        for j, o in enumerate(olist):
                            w = []
                            if j == 0:
                                w.append((f"qW{k % 6}", ld_sw[(g, pi, k)]))
                                if k == 0:
                                    w.append(("ve", ve_zn[g]))
                                    w.append(("ve", ve_epi.get(
                                        (g, pi - 1),
                                        ve_epi.get((g - 1, LASTP), 0))))
                            last = (j == len(olist) - 1)
                            cnt = p.op(
                                "tensor",
                                lambda e, k=k, j=j, o=o, o0=o0, zb=zb:
                                e.matmul(
                                    ps[o - o0][:, 0:G],
                                    swb[:, (k % 6) * 512 + j * 128:
                                        (k % 6) * 512 + (j + 1) * 128],
                                    zn[:, zb + k * G: zb + (k + 1) * G],
                                    start=(k == 0), stop=(k == NCH - 1)),
                                waits=w, inc="pe" if last else None)
                            if last:
                                pe_swc[(g, pi, k)] = cnt
                    pe_pass[(g, pi)] = p.cnt["pe"]
                    for o in olist:
                        p.op("vector",
                             lambda e, o=o, o0=o0: e.scalar_tensor_tensor(
                                 x0f[:, o * G:(o + 1) * G], ps[o - o0][:, 0:G],
                                 c_Gam[:, o:o + 1],
                                 x0f[:, o * G:(o + 1) * G], ALU.mult, ALU.add),
                             waits=([("pe", pe_pass[(g, pi)])]
                                    if o == olist[0] else ()))
                    for o in olist:
                        ve_epi[(g, pi)] = p.op(
                            "vector",
                            lambda e, o=o: e.tensor_scalar(
                                x0f[:, o * G:(o + 1) * G],
                                x0f[:, o * G:(o + 1) * G],
                                c_B2[:, o:o + 1], None, ALU.add),
                            inc="ve" if o == olist[-1] else None)
                sp_x1[g] = p.dma(
                    "sync", v3(d_x1, NCH)[:, :, g * G:(g + 1) * G],
                    v3(x0f[:, :], NCH), waits=[("ve", ve_epi[(g, LASTP)])],
                    sem="qX")
                pe_stt[g], ac_tail[g], _vet = chunk_stats(
                    x0f, castb, sqb, ps[4], ps[5],
                    [("ve", ve_epi[(g, LASTP)]), ("ve", ve_v0m[g])])
                ac_rt, ve_mrt, ve_v0t[g] = rowchain(
                    pe_stt[g], ps[4], ps[5], False, D,
                    extra_w=[("pe", pe_bcm[g])])
                p.op("tensor",
                     lambda e: e.matmul(ps[6][:, 0:G], c_onesrf[0:1, :],
                                        rows[0:1, 2 * G:3 * G],
                                        start=True, stop=True),
                     waits=[("ac", ac_rt), ("ve", ve_mrt),
                            ("ve", vbm[g])])
                pe_bct[g] = p.op(
                    "tensor",
                    lambda e: e.matmul(ps[7][:, 0:G], c_onesrf[0:1, :],
                                       rows[0:1, 3 * G:4 * G],
                                       start=True, stop=True), inc="pe")
                p.op("vector",
                     lambda e: e.tensor_copy(bc[:, 2 * G:3 * G],
                                             ps[6][:, 0:G]),
                     waits=[("pe", pe_bct[g])])
                vbt[g] = p.op("vector",
                              lambda e: e.tensor_copy(bc[:, 3 * G:4 * G],
                                                      ps[7][:, 0:G]),
                              inc="ve")
                p.op("vector",
                     lambda e: e.tensor_copy(bcB[:, G:2 * G],
                                             bc[:, 3 * G:4 * G]))
                p.op("vector",
                     lambda e: e.tensor_tensor(
                         v3(zn1[:, :], NCH), v3(x0f[:, :], NCH),
                         bc[:, 2 * G:3 * G].unsqueeze(1).broadcast_to(
                             (128, NCH, G)), ALU.mult),
                     waits=[("qX", sp_x1[g]), ("qZ", sp_zn1.get(g - 1, 0))])
                ve_zn1[g] = p.op(
                    "vector",
                    lambda e: e.tensor_tensor(
                        v3(zn1[:, :], NCH), v3(zn1[:, :], NCH),
                        bcB[:, G:2 * G].unsqueeze(1).broadcast_to(
                            (128, NCH, G)), ALU.subtract), inc="ve")
                sp_zn1[g] = p.dma(
                    "sync", v3(d_x1n, NCH)[:, :, g * G:(g + 1) * G],
                    v3(zn1[:, :], NCH), waits=[("ve", ve_zn1[g])], sem="qZ")
            a1_x1n = sp_zn1[NG1 - 1]
            a1_x1 = sp_x1[NG1 - 1]

        p.barrier()

        with nc.sbuf_tensor("a2_w", [128, NCH * CQ], BF16) as wqab, \
             nc.sbuf_tensor("a2_x", [128, 2 * NCH * G], BF16) as x1nb, \
             nc.sbuf_tensor("a2_cq", [128, NCQ * G], BF16) as cqb, \
             nc.sbuf_tensor("a2_sq", [128, G], BF16) as cqs, \
             nc.sbuf_tensor("a2_bc", [128, G], F32) as rbc:
            ld_w = p.dma("gpsimd", v3(wqab[:, :], NCH),
                         wqa.rearrange("(c p) t -> p c t", p=128),
                         waits=[("ve", a0_ve)])
            ld_x, pe_cq, ve_ap, sp_cq = {}, {}, {}, {}
            ve_cqd, v0d = {}, {}
            for g in range(NSLOT):
                b = g % 2

                def ldfn(e, g=g, b=b):
                    r = e.value_load(c_offs[0:1, g:g + 1])
                    return e.dma_start(
                        out=v3(x1nb[:, b * NCH * G:(b + 1) * NCH * G], NCH),
                        in_=v3(d_x1n, NCH)[:, :, bass.ts(r, G)])
                ld_x[g] = p.op("sync", ldfn,
                               waits=[("qS", p.cnt["qS"]), ("qZ", a1_x1n),
                                      ("pe", pe_cq.get(g - 2, 0))],
                               inc="qS", inc_k=16)
                for cc in range(NCQ):
                    for k in range(NCH):
                        w = []
                        if k == 0:
                            w = [("qS", ld_x[g]), ("qG", ld_w),
                                 ("ve", ve_ap.get((g - 1, cc), 0)),
                                 ("ve", ve_cqd.get((g, cc - 2), 0))]
                        pe_cq[g] = p.op(
                            "tensor",
                            lambda e, cc=cc, k=k, b=b: e.matmul(
                                ps[cc % 2][:, 0:G],
                                wqab[:, k * CQ + cc * 128:
                                     k * CQ + (cc + 1) * 128],
                                x1nb[:, b * NCH * G + k * G:
                                     b * NCH * G + (k + 1) * G],
                                start=(k == 0), stop=(k == NCH - 1)),
                            waits=w, inc="pe" if k == NCH - 1 else None)
                    ve_cq = p.op("vector",
                                 lambda e, cc=cc: e.tensor_copy(
                                     cqb[:, cc * G:(cc + 1) * G],
                                     ps[cc % 2][:, 0:G]),
                                 waits=[("pe", pe_cq[g])] +
                                       ([("qC", sp_cq.get(g - 1, 0))]
                                        if cc == 0 else []), inc="ve")
                    ve_cqd[(g, cc)] = ve_cq
                    vs = p.op("vector",
                              lambda e, cc=cc: e.tensor_mul(
                                  cqs[:, :], cqb[:, cc * G:(cc + 1) * G],
                                  cqb[:, cc * G:(cc + 1) * G]),
                              waits=[("ve", ve_cq)], inc="ve")
                    pe_ssq = p.op(
                        "tensor",
                        lambda e, cc=cc: e.matmul(
                            ps[2][0:1, 0:G], c_onesb[:, 0:1], cqs[:, :],
                            start=(cc == 0), stop=(cc == NCQ - 1)),
                        waits=[("ve", vs)] +
                              ([("ve", v0d.get(g - 1, 0))] if cc == 0
                               else []),
                        inc="pe" if cc == NCQ - 1 else None)
                v0 = p.op("vector",
                          lambda e: e.tensor_scalar(
                              rows[0:1, 0:G], ps[2][0:1, 0:G], 1.0 / CQ, EPSR,
                              ALU.mult, ALU.add),
                          waits=[("pe", pe_ssq)], inc="ve")
                v0d[g] = v0
                p.op("scalar", lambda e: e.activation(rows[0:1, 0:G],
                                                      rows[0:1, 0:G], AF.Ln),
                     waits=[("ve", v0)])
                a1c = p.op("scalar",
                           lambda e: e.activation(rows[0:1, 0:G],
                                                  rows[0:1, 0:G], AF.Exp,
                                                  scale=-0.5), inc="ac")
                pb = p.op("tensor",
                          lambda e: e.matmul(ps[3][:, 0:G], c_onesrf[0:1, :],
                                             rows[0:1, 0:G], start=True,
                                             stop=True),
                          waits=[("ac", a1c)], inc="pe")
                vb = p.op("vector",
                          lambda e: e.tensor_copy(rbc[:, :], ps[3][:, 0:G]),
                          waits=[("pe", pb)], inc="ve")
                for cc in range(NCQ):
                    ve_ap[(g, cc)] = p.op(
                        "vector",
                        lambda e, cc=cc: e.tensor_tensor(
                            cqb[:, cc * G:(cc + 1) * G],
                            cqb[:, cc * G:(cc + 1) * G], rbc[:, :], ALU.mult),
                        waits=([("ve", vb), ("qC", sp_cq.get(g - 1, 0))]
                               if cc == 0 else ()), inc="ve")
                sp_cq[g] = p.dma("gpsimd",
                                 v3(d_cqn, NCQ)[:, :, g * G:(g + 1) * G],
                                 v3(cqb[:, :], NCQ),
                                 waits=[("ve", ve_ap[(g, NCQ - 1)])],
                                 sem="qC")
            a2a_done = sp_cq[NSLOT - 1]

        p.barrier()

        with nc.sbuf_tensor("b1_w", [128, NCQ * (H * 128)], BF16) as wqnb, \
             nc.sbuf_tensor("b1_c", [128, 2 * NCQ * G], BF16) as cqnb, \
             nc.sbuf_tensor("b1_q", [128, 2 * H * G], BF16) as qng, \
             nc.sbuf_tensor("b1_s", [128, 2 * G], BF16) as qsq:
            ld_w = p.dma("gpsimd", v3(wqnb[:, :], NCQ),
                         wqbn.rearrange("(c p) t -> p c t", p=128))
            ve_wf = 0
            for cc in range(NCQ):
                ve_wf = p.op("vector",
                             lambda e, cc=cc: e.tensor_scalar(
                                 wqnb[:, cc * (H * 128):(cc + 1) * (H * 128)],
                                 wqnb[:, cc * (H * 128):(cc + 1) * (H * 128)],
                                 c_qnw[:, cc:cc + 1], None, ALU.mult),
                             waits=[("qG", ld_w)] if cc == 0 else (),
                             inc="ve")
            ld_c, pe_q, ac_ev, sp_qn, ve_red = {}, {}, {}, {}, {}
            pe_qn = {}
            for g in range(NSLOT):
                b = g % 2
                qh1 = qng[:, b * H * G:(b + 1) * H * G]
                ld_c[g] = p.dma(
                    "sync", v3(cqnb[:, b * NCQ * G:(b + 1) * NCQ * G], NCQ),
                    v3(d_cqn, NCQ)[:, :, g * G:(g + 1) * G],
                    waits=[("qC", a2a_done), ("pe", pe_q.get(g - 2, 0))])
                for h in range(H):
                    for cc in range(NCQ):
                        w = []
                        if cc == 0:
                            w = [("qS", ld_c[g]), ("ve", ve_wf),
                                 ("ac", ac_ev.get((g, h - 2), 0))]
                        pe_q[g] = p.op(
                            "tensor",
                            lambda e, h=h, cc=cc, b=b: e.matmul(
                                ps[h % 2][:, 0:G],
                                wqnb[:, cc * (H * 128) + h * 128:
                                     cc * (H * 128) + (h + 1) * 128],
                                cqnb[:, b * NCQ * G + cc * G:
                                     b * NCQ * G + (cc + 1) * G],
                                start=(cc == 0), stop=(cc == NCQ - 1)),
                            waits=w, inc="pe" if cc == NCQ - 1 else None)
                    ac_ev[(g, h)] = p.op(
                        "scalar",
                        lambda e, h=h, qh1=qh1: e.activation(
                            qh1[:, h * G:(h + 1) * G], ps[h % 2][:, 0:G],
                            AF.Copy, scale=SCALE),
                        waits=[("pe", pe_q[g]),
                               ("qN", sp_qn.get((g - 2, h), 0)),
                               ("ve", ve_red.get((g, h - 1), 0))], inc="ac")
                    hb1 = (h % 2) * G
                    vs = p.op("vector",
                              lambda e, h=h, qh1=qh1, hb1=hb1: e.tensor_mul(
                                  qsq[:, hb1:hb1 + G],
                                  qh1[:, h * G:(h + 1) * G],
                                  qh1[:, h * G:(h + 1) * G]),
                              waits=[("ac", ac_ev[(g, h)]),
                                     ("pe", pe_qn.get((g, h - 2), 0))],
                              inc="ve")
                    pe_qn[(g, h)] = p.op(
                        "tensor",
                        lambda e, hb1=hb1: e.matmul(
                            ps[2][0:1, 0:G], c_onesb[:, 0:1],
                            qsq[:, hb1:hb1 + G], start=True, stop=True),
                        waits=[("ve", vs)], inc="pe")
                    ve_red[(g, h)] = p.op(
                        "vector",
                        lambda e, h=h, g=g: e.reduce_max(
                            qmax[0:1, h * 8 + g * 2: h * 8 + g * 2 + 1],
                            ps[2][0:1, 0:G], axis=AX.X),
                        waits=[("pe", pe_qn[(g, h)])], inc="ve")
                    sp_qn[(g, h)] = p.dma(
                        "gpsimd", v3(d_qn, H)[:, h, g * G:(g + 1) * G],
                        qh1[:, h * G:(h + 1) * G],
                        waits=[("ve", ve_red[(g, h)]),
                               ("qN", p.cnt["qN"] if "qN" in p.cnt else 0)],
                        sem="qN")
            a2b1_done = sp_qn[(NSLOT - 1, H - 1)]

        p.barrier()

        with nc.sbuf_tensor("b2_w", [128, NCQ * (H * DR)], BF16) as wqrb, \
             nc.sbuf_tensor("b2_c", [128, 2 * NCQ * G], BF16) as cqnb2, \
             nc.sbuf_tensor("b2_q", [64, 2 * H * G], BF16) as qrg, \
             nc.sbuf_tensor("b2_q2", [64, H * G], BF16) as qr2, \
             nc.sbuf_tensor("b2_p1", [64, 4 * G], BF16) as rp1, \
             nc.sbuf_tensor("b2_p2", [64, 4 * G], BF16) as rp2, \
             nc.sbuf_tensor("b2_s", [64, 2 * G], BF16) as qsq2:
            ld_w = p.dma("gpsimd", v3(wqrb[:, :], NCQ),
                         wqbr.rearrange("(c p) t -> p c t", p=128))
            ve_wf = 0
            for cc in range(NCQ):
                ve_wf = p.op("vector",
                             lambda e, cc=cc: e.tensor_scalar(
                                 wqrb[:, cc * (H * DR):(cc + 1) * (H * DR)],
                                 wqrb[:, cc * (H * DR):(cc + 1) * (H * DR)],
                                 c_qnw[:, cc:cc + 1], None, ALU.mult),
                             waits=[("qG", ld_w)] if cc == 0 else (),
                             inc="ve")
            ld_c, pe_q, ac_ev, sp_qr, ve_rope = {}, {}, {}, {}, {}
            pe_qr = {}
            vr_done = {}
            for g in range(NSLOT):
                b = g % 2
                qh = qrg[:, b * H * G:(b + 1) * H * G]
                ld_c[g] = p.dma(
                    "sync", v3(cqnb2[:, b * NCQ * G:(b + 1) * NCQ * G], NCQ),
                    v3(d_cqn, NCQ)[:, :, g * G:(g + 1) * G],
                    waits=[("pe", pe_q.get(g - 2, 0))])
                for h in range(H):
                    for cc in range(NCQ):
                        w = []
                        if cc == 0:
                            w = [("qS", ld_c[g]), ("ve", ve_wf),
                                 ("ac", ac_ev.get((g, h - 2), 0)),
                                 (f"qR{g % 2}",
                                  sp_qr.get(g - 2, 0) if h == 0 else 0)]
                        pe_q[g] = p.op(
                            "tensor",
                            lambda e, h=h, cc=cc, b=b: e.matmul(
                                ps[h % 2][0:64, 0:G],
                                wqrb[:, cc * (H * DR) + h * DR:
                                     cc * (H * DR) + (h + 1) * DR],
                                cqnb2[:, b * NCQ * G + cc * G:
                                     b * NCQ * G + (cc + 1) * G],
                                start=(cc == 0), stop=(cc == NCQ - 1)),
                            waits=w, inc="pe" if cc == NCQ - 1 else None)
                    ac_ev[(g, h)] = p.op(
                        "scalar",
                        lambda e, h=h, qh=qh: e.activation(
                            qh[:, h * G:(h + 1) * G], ps[h % 2][0:64, 0:G],
                            AF.Copy, scale=SCALE),
                        waits=[("pe", pe_q[g]),
                               ("ve", vr_done.get(g - 2, 0) if h < 2 else 0)],
                        inc="ac")
                sw1 = p.dma("sync", qr2[0:32, :], qh[32:64, :],
                            waits=[("ac", ac_ev[(g, H - 1)]),
                                   ("ve", ve_rope.get(g - 1, 0))])
                sw2 = p.dma("sync", qr2[32:64, :], qh[0:32, :])
                lastr = 0
                for hq in range(4):
                    sl = slice(hq * 4 * G, (hq + 1) * 4 * G)
                    csl = c_csq[:, g * G:(g + 1) * G].unsqueeze(1)\
                        .broadcast_to((64, 4, G))
                    ssl = c_scq[:, g * G:(g + 1) * G].unsqueeze(1)\
                        .broadcast_to((64, 4, G))
                    qv4 = qh[:, sl].rearrange("p (h t) -> p h t", h=4)
                    qw4 = qr2[:, sl].rearrange("p (h t) -> p h t", h=4)
                    p.op("vector",
                         lambda e, qv4=qv4, csl=csl: e.tensor_tensor(
                             rp1[:, :].rearrange("p (h t) -> p h t", h=4),
                             qv4, csl, ALU.mult),
                         waits=[("qS", sw2)] if hq == 0 else ())
                    p.op("vector",
                         lambda e, qw4=qw4, ssl=ssl: e.tensor_tensor(
                             rp2[:, :].rearrange("p (h t) -> p h t", h=4),
                             qw4, ssl, ALU.mult))
                    lastr = p.op("vector",
                                 lambda e, sl=sl, qh=qh: e.tensor_add(
                                     qh[:, sl], rp1[:, :], rp2[:, :]),
                                 inc="ve")
                ve_rope[g] = lastr
                for h in range(H):
                    hb2 = (h % 2) * G
                    vs = p.op("vector",
                              lambda e, h=h, qh=qh, hb2=hb2: e.tensor_mul(
                                  qsq2[:, hb2:hb2 + G],
                                  qh[:, h * G:(h + 1) * G],
                                  qh[:, h * G:(h + 1) * G]),
                              waits=([("ve", ve_rope[g])] if h <= 1 else
                                     [("pe", pe_qr.get((g, h - 2), 0))]),
                              inc="ve")
                    pe_qr[(g, h)] = p.op(
                        "tensor",
                        lambda e, hb2=hb2: e.matmul(
                            ps[2][0:1, 0:G],
                            c_onesb[0:64, 0:1], qsq2[:, hb2:hb2 + G],
                            start=True, stop=True),
                        waits=[("ve", vs)], inc="pe")
                    p.op("vector",
                         lambda e, h=h, g=g: e.reduce_max(
                             qmax[0:1, h * 8 + g * 2 + 1:h * 8 + g * 2 + 2],
                             ps[2][0:1, 0:G], axis=AX.X),
                         waits=[("pe", pe_qr[(g, h)])], inc="ve")
                vr_done[g] = p.cnt["ve"]
                sp_qr[g] = p.dma(
                    "gpsimd",
                    d_qr.rearrange("p (h t) -> p h t", h=H)[
                        :, :, g * G:(g + 1) * G],
                    qh.rearrange("p (h t) -> p h t", h=H),
                    waits=[("ve", vr_done[g])], sem=f"qR{g % 2}")
            a2b2_done = [("qR0", p.cnt["qR0"]), ("qR1", p.cnt["qR1"])]

        p.barrier()

        NG3 = T // G
        with nc.sbuf_tensor("a3_wa", [128, NCH * CKV], BF16) as wkvab, \
             nc.sbuf_tensor("a3_wr", [128, NCH * DR], BF16) as wkrb, \
             nc.sbuf_tensor("a3_x", [128, 2 * NCH * G], BF16) as x1nb3, \
             nc.sbuf_tensor("a3_cv", [128, NKV * G], BF16) as cvb, \
             nc.sbuf_tensor("a3_s", [128, G], BF16) as cvs, \
             nc.sbuf_tensor("a3_kr", [64, G], BF16) as krg, \
             nc.sbuf_tensor("a3_k2", [64, G], BF16) as kr2g, \
             nc.sbuf_tensor("a3_rp", [64, 2 * G], BF16) as krp, \
             nc.sbuf_tensor("a3_bc", [128, G], F32) as rbc3:
            ld_wa = p.dma("gpsimd", v3(wkvab[:, :], NCH),
                          wkva.rearrange("(c p) t -> p c t", p=128))
            ld_wr = p.dma("gpsimd", v3(wkrb[:, :], NCH),
                          wkr.rearrange("(c p) t -> p c t", p=128))
            ld_x, pe_cv, ve_ap, pe_kr, ac_kr = {}, {}, {}, {}, {}
            swk2 = {}
            ve_cvd, v0d3 = {}, {}
            kr2parts = []
            for g in range(NG3):
                b = g % 2
                ld_x[g] = p.dma(
                    "sync", v3(x1nb3[:, b * NCH * G:(b + 1) * NCH * G], NCH),
                    v3(d_x1n, NCH)[:, :, g * G:(g + 1) * G],
                    waits=[("qZ", a1_x1n), ("pe", pe_kr.get(g - 2, 0))],
                    sem=f"qL3{g % 2}")
                for cc in range(NKV):
                    for k in range(NCH):
                        w = []
                        if k == 0:
                            w = [(f"qL3{g % 2}", ld_x[g]), ("qG", ld_wa),
                                 ("ve", ve_ap.get((g - 1, cc), 0)),
                                 ("ve", ve_cvd.get((g, cc - 2), 0))]
                        pe_cv[g] = p.op(
                            "tensor",
                            lambda e, cc=cc, k=k, b=b: e.matmul(
                                ps[cc % 2][:, 0:G],
                                wkvab[:, k * CKV + cc * 128:
                                      k * CKV + (cc + 1) * 128],
                                x1nb3[:, b * NCH * G + k * G:
                                     b * NCH * G + (k + 1) * G],
                                start=(k == 0), stop=(k == NCH - 1)),
                            waits=w, inc="pe" if k == NCH - 1 else None)
                    ve_cv = p.op("vector",
                                 lambda e, cc=cc: e.tensor_copy(
                                     cvb[:, cc * G:(cc + 1) * G],
                                     ps[cc % 2][:, 0:G]),
                                 waits=[("pe", pe_cv[g])], inc="ve")
                    ve_cvd[(g, cc)] = ve_cv
                    vs = p.op("vector",
                              lambda e, cc=cc: e.tensor_mul(
                                  cvs[:, :], cvb[:, cc * G:(cc + 1) * G],
                                  cvb[:, cc * G:(cc + 1) * G]),
                              waits=[("ve", ve_cv)], inc="ve")
                    pe_ss = p.op(
                        "tensor",
                        lambda e, cc=cc: e.matmul(
                            ps[2][0:1, 0:G], c_onesb[:, 0:1], cvs[:, :],
                            start=(cc == 0), stop=(cc == NKV - 1)),
                        waits=[("ve", vs)] +
                              ([("ve", kr2parts[g - 1] if g > 0 else 0)]
                               if cc == 0 else []),
                        inc="pe" if cc == NKV - 1 else None)
                v0 = p.op("vector",
                          lambda e: e.tensor_scalar(
                              rows[0:1, 0:G], ps[2][0:1, 0:G], 1.0 / CKV,
                              EPSR, ALU.mult, ALU.add),
                          waits=[("pe", pe_ss)], inc="ve")
                v0d3[g] = v0
                p.op("scalar", lambda e: e.activation(rows[0:1, 0:G],
                                                      rows[0:1, 0:G], AF.Ln),
                     waits=[("ve", v0)])
                a1c = p.op("scalar",
                           lambda e: e.activation(rows[0:1, 0:G],
                                                  rows[0:1, 0:G], AF.Exp,
                                                  scale=-0.5), inc="ac")
                pb = p.op("tensor",
                          lambda e: e.matmul(ps[3][:, 0:G], c_onesrf[0:1, :],
                                             rows[0:1, 0:G], start=True,
                                             stop=True),
                          waits=[("ac", a1c)], inc="pe")
                vb = p.op("vector",
                          lambda e: e.tensor_copy(rbc3[:, :], ps[3][:, 0:G]),
                          waits=[("pe", pb)], inc="ve")
                for cc in range(NKV):
                    ve_ap[(g, cc)] = p.op(
                        "vector",
                        lambda e, cc=cc, g=g: e.tensor_tensor(
                            ckvT[:, cc * T + g * G: cc * T + (g + 1) * G],
                            cvb[:, cc * G:(cc + 1) * G], rbc3[:, :], ALU.mult),
                        waits=[("ve", vb)] if cc == 0 else (), inc="ve")
                for k in range(NCH):
                    pe_kr[g] = p.op(
                        "tensor",
                        lambda e, k=k, b=b: e.matmul(
                            ps[3][0:64, 0:G], wkrb[:, k * DR:(k + 1) * DR],
                            x1nb3[:, b * NCH * G + k * G:
                                 b * NCH * G + (k + 1) * G],
                            start=(k == 0), stop=(k == NCH - 1)),
                        waits=([("qG", ld_wr), ("ve", vb),
                                (f"qL3{g % 2}", ld_x[g])] if k == 0 else ()),
                        inc="pe" if k == NCH - 1 else None)
                ac_kr[g] = p.op("scalar",
                                lambda e: e.copy(krg[:, :], ps[3][0:64, 0:G]),
                                waits=[("pe", pe_kr[g]),
                                       ("qS", swk2.get(g - 1, 0))], inc="ac")
                vkr = p.op("vector",
                           lambda e: e.tensor_mul(krp[:, 0:G], krg[:, :],
                                                  krg[:, :]),
                           waits=[("ac", ac_kr[g])], inc="ve")
                pk2 = p.op("tensor",
                           lambda e: e.matmul(ps[2][0:1, 0:G],
                                              c_onesb[0:64, 0:1],
                                              krp[:, 0:G], start=True,
                                              stop=True),
                           waits=[("ve", vkr)], inc="pe")
                kr2parts.append(p.op(
                    "vector",
                    lambda e, g=g: e.reduce_max(kstat[0:1, g:g + 1],
                                                ps[2][0:1, 0:G], axis=AX.X),
                    waits=[("pe", pk2)], inc="ve"))
                swk1 = p.dma("sync", kr2g[0:32, :], krg[32:64, :],
                             waits=[("ac", ac_kr[g]),
                                    ("ve", kr2parts[-1])])
                swk2[g] = p.dma("sync", kr2g[32:64, :], krg[0:32, :])
                csl = c_csk[:, g * G:(g + 1) * G]
                ssl = c_sck[:, g * G:(g + 1) * G]
                p.op("vector",
                     lambda e, csl=csl: e.tensor_tensor(krp[:, 0:G],
                                                        krg[:, :], csl,
                                                        ALU.mult))
                p.op("vector",
                     lambda e, ssl=ssl: e.tensor_tensor(krp[:, G:2 * G],
                                                        kr2g[:, :], ssl,
                                                        ALU.mult),
                     waits=[("qS", swk2[g])])
                p.op("vector",
                     lambda e, g=g: e.tensor_add(
                         krotT[:, g * G:(g + 1) * G],
                         krp[:, 0:G], krp[:, G:2 * G]), inc="ve")
            p.op("vector",
                 lambda e: e.reduce_max(kstat[0:1, 8:9], kstat[0:1, 0:NG3],
                                        axis=AX.X),
                 waits=[("ve", kr2parts[-1])], inc="ve")
            a3_ve = p.cnt["ve"]
            a3_pe = p.cnt["pe"]

        p.barrier()

        with nc.sbuf_tensor("b_wb", [128, NKV * (H * 256)], BF16) as wkvbb, \
             nc.sbuf_tensor("b_msk", [128, 32 * G], BF16) as msks, \
             nc.sbuf_tensor("b_kp", [128, 2 * T], BF16) as kpass, \
             nc.sbuf_tensor("b_v", [128, 2 * T], BF16) as vkm, \
             nc.sbuf_tensor("b_qn", [128, 2 * OWN], BF16) as qnh, \
             nc.sbuf_tensor("b_qr", [64, 2 * OWN], BF16) as qrh, \
             nc.sbuf_tensor("b_pt", [128, 2 * G], BF16) as ptb, \
             nc.sbuf_tensor("b_s", [128, G], BF16) as ksq, \
             nc.sbuf_tensor("b_vs", [128, 2 * G], BF16) as vsb, \
             nc.sbuf_tensor("b_ot", [128, 2 * G], BF16) as otb, \
             nc.sbuf_tensor("b_rl", [128, 2 * G], F32) as rlb, \
             nc.sbuf_tensor("b_mb", [128, 4], F32) as mbt:
            ld_wb = p.dma("gpsimd", v3(wkvbb[:, :], NKV),
                          wkvb.rearrange("(c p) t -> p c t", p=128),
                          waits=[("ve", a3_ve)])
            ld_msk = p.dma("sync", msks[:, :], mskin)
            ve_wf = 0
            for cc in range(NKV):
                ve_wf = p.op("vector",
                             lambda e, cc=cc: e.tensor_scalar(
                                 wkvbb[:, cc * (H * 256):
                                       (cc + 1) * (H * 256)],
                                 wkvbb[:, cc * (H * 256):
                                       (cc + 1) * (H * 256)],
                                 c_kvnw[:, cc:cc + 1], None, ALU.mult),
                             waits=[("qG", ld_wb)] if cc == 0 else (),
                             inc="ve")
            ld_qn, ld_qr, pe_hd, ve_hd, sp_ot = {}, {}, {}, {}, {}
            for h in range(H):
                hb = h % 2
                ld_qn[h] = p.dma("sync", qnh[:, hb * OWN:(hb + 1) * OWN],
                                 v3(d_qn, H)[:, h, :],
                                 waits=[("qN", a2b1_done),
                                        ("pe", pe_hd.get(h - 2, 0))],
                                 sem=f"qQN{h % 2}")
                ld_qr[h] = p.dma("sync", qrh[:, hb * OWN:(hb + 1) * OWN],
                                 d_qr.rearrange("p (h t) -> p h t", h=H)[
                                     :, h, :],
                                 waits=list(a2b2_done) +
                                       [("pe", pe_hd.get(h - 2, 0))],
                                 sem=f"qQR{h % 2}")
                pe_kg, ac_kg, kp2r = {}, {}, {}
                pe_k2 = {}
                for kg in range(T // G):
                    for cc in range(NKV):
                        pe_kg[kg] = p.op(
                            "tensor",
                            lambda e, kg=kg, cc=cc, h=h: e.matmul(
                                ps[6 + kg % 2][:, 0:G],
                                wkvbb[:, cc * (H * 256) + h * 256:
                                      cc * (H * 256) + h * 256 + 128],
                                ckvT[:, cc * T + kg * G:
                                     cc * T + (kg + 1) * G],
                                start=(cc == 0), stop=(cc == NKV - 1)),
                            waits=([("ve", ve_wf),
                                    ("ac", ac_kg.get(kg - 2, 0)),
                                    ("ve", ve_hd.get(h - 1, 0)),
                                    ("pe", a3_pe if (h == 0 and kg == 0)
                                     else 0)]
                                   if cc == 0 else ()),
                            inc="pe" if cc == NKV - 1 else None)
                    ac_kg[kg] = p.op(
                        "scalar",
                        lambda e, kg=kg, hb=hb: e.copy(
                            kpass[:, hb * T + kg * G: hb * T + (kg + 1) * G],
                            ps[6 + kg % 2][:, 0:G]),
                        waits=[("pe", pe_kg[kg])], inc="ac")
                    vs = p.op("vector",
                              lambda e, kg=kg, hb=hb: e.tensor_mul(
                                  ksq[:, :],
                                  kpass[:, hb * T + kg * G:
                                        hb * T + (kg + 1) * G],
                                  kpass[:, hb * T + kg * G:
                                        hb * T + (kg + 1) * G]),
                              waits=[("ac", ac_kg[kg]),
                                     ("pe", pe_k2.get(kg - 1, 0))], inc="ve")
                    pe_k2[kg] = p.op(
                        "tensor",
                        lambda e: e.matmul(ps[2][0:1, 0:G], c_onesb[:, 0:1],
                                           ksq[:, :], start=True, stop=True),
                        waits=[("ve", vs)], inc="pe")
                    kp2r[kg] = p.op(
                        "vector",
                        lambda e, kg=kg: e.reduce_max(
                            kstat[0:1, 16 + kg:17 + kg], ps[2][0:1, 0:G],
                            axis=AX.X),
                        waits=[("pe", pe_k2[kg])], inc="ve")
                vk1 = p.op("vector",
                           lambda e: e.reduce_max(kstat[0:1, 24:25],
                                                  kstat[0:1, 16:24],
                                                  axis=AX.X),
                           waits=[("ve", kp2r[T // G - 1])], inc="ve")
                vk2 = p.op("vector",
                           lambda e: e.tensor_add(kstat[0:1, 25:26],
                                                  kstat[0:1, 24:25],
                                                  kstat[0:1, 8:9]),
                           waits=[("ve", vk1)], inc="ve")
                p.op("scalar",
                     lambda e: e.activation(kstat[0:1, 25:26],
                                            kstat[0:1, 25:26], AF.Ln),
                     waits=[("ve", vk2)])
                ak = p.op("scalar",
                          lambda e: e.activation(kstat[0:1, 25:26],
                                                 kstat[0:1, 25:26], AF.Exp,
                                                 scale=0.5), inc="ac")
                mb_done = {}
                for sl_ in range(NSLOT):
                    v1 = p.op("vector",
                              lambda e, h=h, sl_=sl_: e.tensor_add(
                                  kstat[0:1, 32 + sl_:33 + sl_],
                                  qmax[0:1, h * 8 + sl_ * 2:
                                       h * 8 + sl_ * 2 + 1],
                                  qmax[0:1, h * 8 + sl_ * 2 + 1:
                                       h * 8 + sl_ * 2 + 2]),
                              waits=[("ac", ak)], inc="ve")
                    p.op("scalar",
                         lambda e, sl_=sl_: e.activation(
                             kstat[0:1, 32 + sl_:33 + sl_],
                             kstat[0:1, 32 + sl_:33 + sl_], AF.Ln),
                         waits=[("ve", v1)])
                    a2c = p.op("scalar",
                               lambda e, sl_=sl_: e.activation(
                                   kstat[0:1, 32 + sl_:33 + sl_],
                                   kstat[0:1, 32 + sl_:33 + sl_], AF.Exp,
                                   scale=0.5), inc="ac")
                    v2 = p.op("vector",
                              lambda e, sl_=sl_: e.tensor_mul(
                                  kstat[0:1, 40 + sl_:41 + sl_],
                                  kstat[0:1, 32 + sl_:33 + sl_],
                                  kstat[0:1, 25:26]),
                              waits=[("ac", a2c)], inc="ve")
                    v2 = p.op("vector",
                              lambda e, sl_=sl_: e.tensor_scalar(
                                  kstat[0:1, 40 + sl_:41 + sl_],
                                  kstat[0:1, 40 + sl_:41 + sl_], -1.0, None,
                                  ALU.mult), inc="ve")
                    pb = p.op("tensor",
                              lambda e, sl_=sl_: e.matmul(
                                  ps[2][:, sl_:sl_ + 1], c_onesrf[0:1, :],
                                  kstat[0:1, 40 + sl_:41 + sl_],
                                  start=True, stop=True),
                              waits=[("ve", v2)], inc="pe")
                    mb_done[sl_] = p.op(
                        "vector",
                        lambda e, sl_=sl_: e.tensor_copy(
                            mbt[:, sl_:sl_ + 1], ps[2][:, sl_:sl_ + 1]),
                        waits=[("pe", pb)], inc="ve")
                ve_vg = {}
                pe_vg = {}
                for k4 in range(T // G):
                    for j in range(4):
                        kt = k4 * 4 + j
                        for cc in range(NKV):
                            pe_vg[k4] = p.op(
                                "tensor",
                                lambda e, kt=kt, j=j, cc=cc, h=h, k4=k4:
                                e.matmul(
                                    ps[6 + k4 % 2][:, j * 128:(j + 1) * 128],
                                    ckvT[:, cc * T + kt * 128:
                                         cc * T + (kt + 1) * 128],
                                    wkvbb[:, cc * (H * 256) + h * 256 + 128:
                                          cc * (H * 256) + (h + 1) * 256],
                                    start=(cc == 0 and j == 0),
                                    stop=(cc == NKV - 1 and j == 3)),
                                waits=([("ve", ve_vg.get(k4 - 2, 0)),
                                        ("ac", ac_kg[T // G - 1])]
                                       if (j == 0 and cc == 0) else ()),
                                inc="pe" if (cc == NKV - 1 and j == 3)
                                else None)
                    ve_vg[k4] = p.op(
                        "vector",
                        lambda e, k4=k4, hb=hb: e.tensor_copy(
                            vkm[:, hb * T + k4 * G: hb * T + (k4 + 1) * G],
                            ps[6 + k4 % 2][:, 0:G]),
                        waits=[("pe", pe_vg[k4])], inc="ve")
                ve_vg_done = ve_vg[T // G - 1]

                ve_ot = {}
                vb_prev = 0
                last_exp_prev = 0
                for sl_ in range(NSLOT):
                    nkt = NKT[sl_]
                    reg0 = nkt - 8
                    ac_exp, ve_msk, pe_lpv, pe_sc = {}, {}, {}, {}

                    def emit_lpv(i, sl_=sl_, nkt=nkt, ac_exp=None,
                                 ve_msk=None, pe_lpv=None, ve_ot=ve_ot, h=h,
                                 hb=hb):
                        w = [("ve", ve_msk[i])] if i in ve_msk else \
                            [("ac", ac_exp[i])]
                        if i == 0:
                            w.append(("ve", ve_ot.get(sl_ - 2, 0)))
                            if sl_ < 2:
                                w.append(("ve", ve_hd.get(h - 1, 0)))
                        p.op("tensor",
                             lambda e, i=i, sl_=sl_, nkt=nkt: e.matmul(
                                 ps[2 + sl_ % 2][0:1, 0:G], c_onesb[:, 0:1],
                                 ptb[:, (i % 2) * G:(i % 2 + 1) * G],
                                 start=(i == 0), stop=(i == nkt - 1)),
                             waits=w)
                        pe_lpv[i] = p.op(
                            "tensor",
                            lambda e, i=i, sl_=sl_, nkt=nkt, hb=hb: e.matmul(
                                ps[4 + sl_ % 2][:, 0:G],
                                vkm[:, hb * T + i * 128:
                                    hb * T + (i + 1) * 128],
                                ptb[:, (i % 2) * G:(i % 2 + 1) * G],
                                start=(i == 0), stop=(i == nkt - 1)),
                            inc="pe")

                    for i in range(nkt):
                        w = [("ac", ac_exp.get(i - 2, 0))]
                        if i == 0:
                            w += [("ac", ac_kg[T // G - 1]),
                                  (f"qQN{h % 2}", ld_qn[h]),
                                  (f"qQR{h % 2}", ld_qr[h]),
                                  ("ve", mb_done[NSLOT - 1]),
                                  ("ve", ve_vg_done),
                                  ("ac", last_exp_prev),
                                  ("ve", vb_prev)]
                        p.op("tensor",
                             lambda e, i=i, hb=hb, sl_=sl_: e.matmul(
                                 ps[i % 2][:, 0:G],
                                 kpass[:, hb * T + i * 128:
                                       hb * T + (i + 1) * 128],
                                 qnh[:, hb * OWN + sl_ * G:
                                     hb * OWN + (sl_ + 1) * G],
                                 start=True, stop=False),
                             waits=w)
                        pe_sc[i] = p.op(
                            "tensor",
                            lambda e, i=i, hb=hb, sl_=sl_: e.matmul(
                                ps[i % 2][:, 0:G],
                                krotT[:, i * 128:(i + 1) * 128],
                                qrh[:, hb * OWN + sl_ * G:
                                    hb * OWN + (sl_ + 1) * G],
                                start=False, stop=True),
                            inc="pe")
                        if i >= 1:
                            emit_lpv(i - 1, ac_exp=ac_exp, ve_msk=ve_msk,
                                     pe_lpv=pe_lpv)
                        ac_exp[i] = p.op(
                            "scalar",
                            lambda e, i=i, sl_=sl_: e.activation(
                                ptb[:, (i % 2) * G:(i % 2 + 1) * G],
                                ps[i % 2][:, 0:G], AF.Exp,
                                bias=mbt[:, sl_:sl_ + 1]),
                            waits=[("pe", pe_sc[i]),
                                   ("pe", pe_lpv.get(i - 2, 0))], inc="ac")
                        if i >= reg0:
                            j = i - reg0
                            ve_msk[i] = p.op(
                                "vector",
                                lambda e, i=i, j=j, sl_=sl_: e.tensor_mul(
                                    ptb[:, (i % 2) * G:(i % 2 + 1) * G],
                                    ptb[:, (i % 2) * G:(i % 2 + 1) * G],
                                    msks[:, (sl_ * 8 + j) * G:
                                         (sl_ * 8 + j + 1) * G]),
                                waits=[("ac", ac_exp[i]),
                                       ("qS", ld_msk if h == 0 else 0)],
                                inc="ve")
                    emit_lpv(nkt - 1, ac_exp=ac_exp, ve_msk=ve_msk,
                             pe_lpv=pe_lpv)
                    last_exp_prev = ac_exp[nkt - 1]
                    lpv_end = p.cnt["pe"]
                    vr = p.op("vector",
                              lambda e, sl_=sl_: e.reciprocal(
                                  rows[0:1, 0:G], ps[2 + sl_ % 2][0:1, 0:G]),
                              waits=[("pe", lpv_end)], inc="ve")
                    pb = p.op("tensor",
                              lambda e, sl_=sl_: e.matmul(
                                  ps[sl_ % 2][:, 0:G], c_onesrf[0:1, :],
                                  rows[0:1, 0:G], start=True, stop=True),
                              waits=[("ve", vr)], inc="pe")
                    vb_prev = p.op(
                        "vector",
                        lambda e, sl_=sl_: e.tensor_copy(
                            rlb[:, (sl_ % 2) * G:(sl_ % 2 + 1) * G],
                            ps[sl_ % 2][:, 0:G]),
                        waits=[("pe", pb)], inc="ve")
                    ve_ot[sl_] = p.op(
                        "vector",
                        lambda e, sl_=sl_: e.tensor_tensor(
                            otb[:, (sl_ % 2) * G:(sl_ % 2 + 1) * G],
                            ps[4 + sl_ % 2][:, 0:G],
                            rlb[:, (sl_ % 2) * G:(sl_ % 2 + 1) * G],
                            ALU.mult),
                        waits=[(f"qO{sl_}", sp_ot.get((h - 1, sl_), 0))],
                        inc="ve")
                    sp_ot[(h, sl_)] = p.dma(
                        "gpsimd", v3(d_ot, H)[:, h, sl_ * G:(sl_ + 1) * G],
                        otb[:, (sl_ % 2) * G:(sl_ % 2 + 1) * G],
                        waits=[("ve", ve_ot[sl_])], sem=f"qO{sl_}")
                pe_hd[h] = p.cnt["pe"]
                ve_hd[h] = p.cnt["ve"]
            b_done = [(f"qO{s}", p.cnt[f"qO{s}"]) for s in range(NSLOT)]

        p.barrier()

        with nc.sbuf_tensor("c1_wo", [128, 12 * 512], BF16) as wob, \
             nc.sbuf_tensor("c1_ot", [128, NCH * G], BF16) as otg, \
             nc.sbuf_tensor("c1_x1", [128, 2 * NCH * G], F32) as x1s, \
             nc.sbuf_tensor("c1_cb", [128, G], BF16) as castb2, \
             nc.sbuf_tensor("c1_sq", [128, G], BF16) as sqb2, \
             nc.sbuf_tensor("c1_zn", [128, NCH * G], BF16) as zn2, \
             nc.sbuf_tensor("c1_bc", [128, 2 * G], F32) as bc2, \
             nc.sbuf_tensor("c1_bB", [128, G], BF16) as bcB2:
            wo_v = wo.rearrange("(c p) t -> p c t", p=128)
            ld_ot, ld_x1, ld_wo = {}, {}, {}
            pe_woc, pe_pass2, ve_epi2, ve_zn2, ve_v02 = {}, {}, {}, {}, {}
            pe_st2, ac_c2, sp_x2, sp_zn2, pb2 = {}, {}, {}, {}, {}
            vb2 = {}
            for g in range(NSLOT):
                ld_ot[g] = p.dma(
                    "sync", v3(otg[:, :], NCH),
                    v3(d_ot, NCH)[:, :, g * G:(g + 1) * G],
                    waits=list(b_done) +
                          [("pe", pe_pass2.get((g - 1, LASTP), 0))])

                x1h = x1s[:, (g % 2) * NCH * G:(g % 2 + 1) * NCH * G]

                def ldx1(e, g=g, x1h=x1h):
                    r = e.value_load(c_offs[0:1, g:g + 1])
                    return e.dma_start(out=v3(x1h, NCH),
                                       in_=v3(d_x1, NCH)[:, :, bass.ts(r, G)])
                ld_x1[g] = p.op("sync", ldx1,
                                waits=[("qS", p.cnt["qS"]), ("qX", a1_x1),
                                       (f"qX2{g % 2}", sp_x2.get(g - 2, 0)),
                                       ("ac", ac_c2.get(g - 2, 0)),
                                       ("ve", ve_zn2.get(g - 2, 0))],
                                inc="qS", inc_k=16)
                for pi, olist in enumerate(OP3):
                    o0 = olist[0]
                    for hv in range(NCH):
                        if hv < 12:
                            pk = hv + 12 if hv + 12 <= 15 else hv
                            wv = pe_woc.get((g, pi - 1, pk),
                                            pe_woc.get((g - 1, LASTP, pk), 0))
                        else:
                            wv = pe_woc[(g, pi, hv - 12)]
                        ld_wo[(g, pi, hv)] = p.dma(
                            "gpsimd",
                            wob[:, (hv % 12) * 512:
                                (hv % 12) * 512 + len(olist) * 128],
                            wo_v[:, hv, o0 * 128:(o0 + len(olist)) * 128],
                            waits=[("pe", wv)], sem=f"qV{hv % 12}")
                        for j, o in enumerate(olist):
                            w = []
                            if j == 0:
                                w.append((f"qV{hv % 12}", ld_wo[(g, pi, hv)]))
                                if hv == 0:
                                    w.append(("qS", ld_ot[g]))
                                    w.append(("ve", ve_epi2.get(
                                        (g, pi - 1),
                                        ve_epi2.get((g - 1, LASTP), 0))))
                            last = (j == len(olist) - 1)
                            cnt = p.op(
                                "tensor",
                                lambda e, hv=hv, j=j, o=o, o0=o0: e.matmul(
                                    ps[o - o0][:, 0:G],
                                    wob[:, (hv % 12) * 512 + j * 128:
                                        (hv % 12) * 512 + (j + 1) * 128],
                                    otg[:, hv * G:(hv + 1) * G],
                                    start=(hv == 0), stop=(hv == NCH - 1)),
                                waits=w, inc="pe" if last else None)
                            if last:
                                pe_woc[(g, pi, hv)] = cnt
                    pe_pass2[(g, pi)] = p.cnt["pe"]
                    for o in olist:
                        ve_epi2[(g, pi)] = p.op(
                            "vector",
                            lambda e, o=o, o0=o0, x1h=x1h: e.scalar_tensor_tensor(
                                x1h[:, o * G:(o + 1) * G], ps[o - o0][:, 0:G],
                                1.0, x1h[:, o * G:(o + 1) * G],
                                ALU.mult, ALU.add),
                            waits=([("pe", pe_pass2[(g, pi)]),
                                    ("qS", ld_x1[g])] if o == olist[0]
                                   else ()),
                            inc="ve" if o == olist[-1] else None)
                sp_x2[g] = p.dma(
                    "scalar", v3(d_x2, NCH)[:, :, g * G:(g + 1) * G],
                    v3(x1h, NCH), waits=[("ve", ve_epi2[(g, LASTP)])],
                    sem=f"qX2{g % 2}")
                pe_st2[g], ac_c2[g], _ = chunk_stats(
                    x1h, castb2, sqb2, ps[4], ps[5],
                    [("ve", ve_epi2[(g, LASTP)]), ("ve", ve_v02.get(g - 1, 0))])
                ac_r2, ve_mr2, ve_v02[g] = rowchain(
                    pe_st2[g], ps[4], ps[5], True, D,
                    extra_w=[("pe", pb2.get(g - 1, 0))])
                p.op("tensor",
                     lambda e: e.matmul(ps[6][:, 0:G], c_onesrf[0:1, :],
                                        rows[0:1, 2 * G:3 * G], start=True,
                                        stop=True),
                     waits=[("ac", ac_r2), ("ve", ve_mr2),
                            ("ve", vb2.get(g - 1, 0))])
                pb2[g] = p.op("tensor",
                              lambda e: e.matmul(ps[7][:, 0:G],
                                                 c_onesrf[0:1, :],
                                                 rows[0:1, 3 * G:4 * G],
                                                 start=True, stop=True),
                              inc="pe")
                p.op("vector",
                     lambda e: e.tensor_copy(bc2[:, 0:G], ps[6][:, 0:G]),
                     waits=[("pe", pb2[g])])
                vb2[g] = p.op("vector",
                              lambda e: e.tensor_copy(bc2[:, G:2 * G],
                                                      ps[7][:, 0:G]),
                              inc="ve")
                p.op("vector",
                     lambda e: e.tensor_copy(bcB2[:, 0:G], bc2[:, G:2 * G]))
                p.op("vector",
                     lambda e, x1h=x1h: e.tensor_tensor(
                         v3(zn2[:, :], NCH), v3(x1h, NCH),
                         bc2[:, 0:G].unsqueeze(1).broadcast_to((128, NCH, G)),
                         ALU.mult),
                     waits=[(f"qX2{g % 2}", sp_x2[g]),
                            ("qZ2", sp_zn2.get(g - 1, 0))])
                p.op("vector",
                     lambda e: e.tensor_tensor(
                         v3(zn2[:, :], NCH), v3(zn2[:, :], NCH),
                         bcB2[:, 0:G].unsqueeze(1).broadcast_to(
                             (128, NCH, G)), ALU.subtract))
                for c in range(NCH):
                    ve_zn2[g] = p.op(
                        "vector",
                        lambda e, c=c: e.tensor_scalar(
                            zn2[:, c * G:(c + 1) * G],
                            zn2[:, c * G:(c + 1) * G],
                            c_lnw[:, c:c + 1], c_lnb[:, c:c + 1],
                            ALU.mult, ALU.add),
                        inc="ve" if c == NCH - 1 else None)
                sp_zn2[g] = p.dma(
                    "scalar", v3(d_zn2, NCH)[:, :, g * G:(g + 1) * G],
                    v3(zn2[:, :], NCH), waits=[("ve", ve_zn2[g])], sem="qZ2")
            c1_done = sp_zn2[NSLOT - 1]
            c1_x2 = [("qX20", p.cnt["qX20"]), ("qX21", p.cnt["qX21"])]

        p.barrier()

        with nc.sbuf_tensor("c2_sw", [128, 6 * 512], BF16) as swb2, \
             nc.sbuf_tensor("c2_zn", [128, 2 * NCH * G], BF16) as znc, \
             nc.sbuf_tensor("c2_x2", [128, 2 * NCH * G], F32) as x2s:
            swv = v3(d_sw, NCH)
            ld_zn, ld_x2, ld_sw2 = {}, {}, {}
            pe_swc2, pe_pass3, ve_epi3, sp_y = {}, {}, {}, {}
            for g in range(NSLOT):
                znh = znc[:, (g % 2) * NCH * G:(g % 2 + 1) * NCH * G]
                x2h = x2s[:, (g % 2) * NCH * G:(g % 2 + 1) * NCH * G]
                ld_zn[g] = p.dma(
                    "sync", v3(znh, NCH),
                    v3(d_zn2, NCH)[:, :, g * G:(g + 1) * G],
                    waits=[("qZ2", c1_done),
                           ("pe", pe_pass3.get((g - 2, LASTP), 0))])
                ld_x2[g] = p.dma(
                    "sync", v3(x2h, NCH),
                    v3(d_x2, NCH)[:, :, g * G:(g + 1) * G],
                    waits=list(c1_x2) +
                          [(f"qY{g % 2}", sp_y.get(g - 2, 0)),
                           ("ve", ve_epi3.get((g - 2, LASTP), 0))])
                for pi, olist in enumerate(OP3):
                    o0 = olist[0]
                    for k in range(NCH):
                        pk = _prev_chunk_user(k) if k < 6 else None
                        if pk is not None:
                            wv = pe_swc2.get((g, pi - 1, pk),
                                             pe_swc2.get((g - 1, LASTP, pk), 0))
                        else:
                            wv = pe_swc2[(g, pi, k - 6)]
                        ld_sw2[(g, pi, k)] = p.dma(
                            "scalar",
                            swb2[:, (k % 6) * 512:
                                 (k % 6) * 512 + len(olist) * 128],
                            swv[:, k, o0 * 128:(o0 + len(olist)) * 128],
                            waits=[("pe", wv)], sem=f"qW{k % 6}")
                        for j, o in enumerate(olist):
                            w = []
                            if j == 0:
                                w.append((f"qW{k % 6}", ld_sw2[(g, pi, k)]))
                                if k == 0:
                                    w.append(("qS", ld_zn[g]))
                                    w.append(("ve", ve_epi3.get(
                                        (g, pi - 1),
                                        ve_epi3.get((g - 1, LASTP), 0))))
                            last = (j == len(olist) - 1)
                            cnt = p.op(
                                "tensor",
                                lambda e, k=k, j=j, o=o, o0=o0, znh=znh:
                                e.matmul(
                                    ps[o - o0][:, 0:G],
                                    swb2[:, (k % 6) * 512 + j * 128:
                                         (k % 6) * 512 + (j + 1) * 128],
                                    znh[:, k * G:(k + 1) * G],
                                    start=(k == 0), stop=(k == NCH - 1)),
                                waits=w, inc="pe" if last else None)
                            if last:
                                pe_swc2[(g, pi, k)] = cnt
                    pe_pass3[(g, pi)] = p.cnt["pe"]
                    for o in olist:
                        p.op("vector",
                             lambda e, o=o, o0=o0, x2h=x2h:
                             e.scalar_tensor_tensor(
                                 x2h[:, o * G:(o + 1) * G],
                                 ps[o - o0][:, 0:G], c_Gam[:, o:o + 1],
                                 x2h[:, o * G:(o + 1) * G],
                                 ALU.mult, ALU.add),
                             waits=([("pe", pe_pass3[(g, pi)]),
                                     ("qS", ld_x2[g])] if o == olist[0]
                                    else ()))
                    for o in olist:
                        ve_epi3[(g, pi)] = p.op(
                            "vector",
                            lambda e, o=o, x2h=x2h: e.tensor_scalar(
                                x2h[:, o * G:(o + 1) * G],
                                x2h[:, o * G:(o + 1) * G],
                                c_B2[:, o:o + 1], None, ALU.add),
                            inc="ve" if o == olist[-1] else None)
                sp_y[g] = p.dma(
                    "gpsimd", v3(yT, NCH)[:, :, g * G:(g + 1) * G],
                    v3(x2h, NCH), waits=[("ve", ve_epi3[(g, LASTP)])],
                    sem=f"qY{g % 2}")
            p.op("gpsimd", lambda e: None,
                 waits=[("qY0", p.cnt["qY0"]), ("qY1", p.cnt["qY1"])])

        with nc.Block() as block:
            @block.sync
            def _(eng):
                for f in p.ops["sync"]:
                    f(eng)

            @block.gpsimd
            def _(eng):
                for f in p.ops["gpsimd"]:
                    f(eng)

            @block.scalar
            def _(eng):
                for f in p.ops["scalar"]:
                    f(eng)

            @block.vector
            def _(eng):
                for f in p.ops["vector"]:
                    f(eng)

            @block.tensor
            def _(eng):
                for f in p.ops["tensor"]:
                    f(eng)

    return nc


_PROG = []


def _get_program():
    if not _PROG:
        _PROG.append(build_program())
    return _PROG[0]


def _rope_tables(positions):
    inv = 1.0 / 10000.0 ** (np.arange(0, 64, 2, dtype=np.float64) / 64.0)
    fr = np.outer(positions.astype(np.float64), inv)
    c = np.cos(fr).astype(np.float32).T
    s = np.sin(fr).astype(np.float32).T
    cc = np.ascontiguousarray(np.concatenate([c, c], 0)).astype(BF)
    ssp = np.ascontiguousarray(np.concatenate([-s, s], 0)).astype(BF)
    return cc, ssp


def _masks(blocks):
    m = np.zeros((128, 32 * G), BF)
    for s in range(NSLOT):
        nkt = NKT[s]
        q0 = blocks[s] * G
        for j in range(8):
            kt = nkt - 8 + j
            kk = kt * 128 + np.arange(128)[:, None]
            qq = q0 + np.arange(G)[None, :]
            m[:, (s * 8 + j) * G:(s * 8 + j + 1) * G] = \
                (kk <= qq).astype(BF)
    return m


def kernel(L0_attn_hidden_states, L0_attn_Wq_a, L0_attn_q_norm_w,
           L0_attn_Wq_b, L0_attn_Wk_rope, L0_attn_Wkv_a, L0_attn_kv_norm_w,
           L0_attn_Wkv_b, L0_attn_Wo, L0_ffn_weight, L0_ffn_gamma,
           L0_ffn_beta, L0_ffn_ln_weight, L0_ffn_ln_bias):
    f32 = np.float32
    hs = np.asarray(L0_attn_hidden_states, f32)

    def pcol(v, nch):
        return np.ascontiguousarray(np.asarray(v, f32).reshape(nch, 128).T)

    wqb = np.asarray(L0_attn_Wq_b, f32).reshape(CQ, H, 192)
    shared = {
        "wfT": np.ascontiguousarray(np.asarray(L0_ffn_weight, f32).T),
        "wqa": np.asarray(L0_attn_Wq_a, f32),
        "wqbn": np.ascontiguousarray(wqb[:, :, :128].reshape(CQ, H * 128)),
        "wqbr": np.ascontiguousarray(wqb[:, :, 128:].reshape(CQ, H * DR)),
        "wkva": np.asarray(L0_attn_Wkv_a, f32),
        "wkvb": np.asarray(L0_attn_Wkv_b, f32),
        "wkr": np.asarray(L0_attn_Wk_rope, f32),
        "wo": np.asarray(L0_attn_Wo, f32),
        "gamma_t": pcol(L0_ffn_gamma, NCH),
        "beta_t": pcol(L0_ffn_beta, NCH),
        "lnw_t": pcol(L0_ffn_ln_weight, NCH),
        "qnw_t": pcol(L0_attn_q_norm_w, NCQ),
        "kvnw_t": pcol(L0_attn_kv_norm_w, NKV),
        "ones_cb": np.ones((128, 1), BF),
        "ones_rf": np.ones((1, 128), f32),
        "ident_f": np.eye(128, dtype=f32),
    }
    shared["lnb_t"] = pcol(L0_ffn_ln_bias, NCH)
    cs_k, sc_k = _rope_tables(np.arange(T))
    shared["cs_k"] = cs_k
    shared["sc_k"] = sc_k

    nc = _get_program()
    in_maps = []
    for core in range(8):
        bidx, half = core // 2, core % 2
        blocks = BLOCKS[half]
        pos = np.concatenate([np.arange(bb * G, (bb + 1) * G)
                              for bb in blocks])
        cs_q, sc_q = _rope_tables(pos)
        m = dict(shared)
        m["cs_q"] = cs_q
        m["sc_q"] = sc_q
        m["x0T"] = np.ascontiguousarray(hs[bidx].T)
        m["mskin"] = _masks(blocks)
        off = np.zeros((1, 8), np.int32)
        off[0, :NSLOT] = blocks
        m["offs"] = off
        in_maps.append(m)

    res = run_bass_kernel_spmd(nc, in_maps, core_ids=list(range(8)))

    out = np.empty((B, T, D), f32)
    for core in range(8):
        bidx, half = core // 2, core % 2
        blocks = BLOCKS[half]
        y = np.asarray(res.results[core]["yT"]).reshape(128, NCH, OWN)
        yt = y.transpose(2, 1, 0).reshape(OWN, D)
        for s, bb in enumerate(blocks):
            out[bidx, bb * G:(bb + 1) * G, :] = yt[s * G:(s + 1) * G, :]
    return out


# version 15
# speedup vs baseline: 1.0186x; 1.0186x over previous
import numpy as np
import ml_dtypes
from contextlib import ExitStack

import concourse.bass as bass
import concourse.mybir as mybir
from concourse.bass_utils import run_bass_kernel_spmd

F32 = mybir.dt.float32
BF16 = mybir.dt.bfloat16
I32 = mybir.dt.int32
AF = mybir.ActivationFunctionType
ALU = mybir.AluOpType
AX = mybir.AxisListType

B, T, D = 4, 4096, 2048
H, DR = 16, 64
CQ, CKV = 1536, 512
OWN = 2048
NCH = D // 128
NCQ = CQ // 128
NKV = CKV // 128
G = 512
NSLOT = OWN // G
NKT = [8, 16, 24, 32]
SCALE = float(1.0 / np.sqrt(192.0))
EPS1, EPS2, EPSR = 1e-5, 1e-5, 1e-6
BF = ml_dtypes.bfloat16
BLOCKS = {0: [0, 3, 5, 6], 1: [1, 2, 4, 7]}
OP3 = [list(range(4 * i, 4 * i + 4)) for i in range(4)]
LASTP = 3

ENGINES = ("sync", "gpsimd", "scalar", "vector", "tensor")


def _prev_chunk_user(k):
    return k + 12 if k + 12 <= 15 else k + 6



class Prog:
    def __init__(self, nc, ctx):
        self.nc = nc
        self.ctx = ctx
        self.ops = {e: [] for e in ENGINES}
        self.sem = {}
        self.cnt = {}
        for name in ("qS", "qG", "qA", "pe", "ve", "ac"):
            self.sem[name] = ctx.enter_context(nc.semaphore(name))
            self.cnt[name] = 0

    def new_sem(self, name):
        if name not in self.sem:
            self.sem[name] = self.ctx.enter_context(self.nc.semaphore(name))
            self.cnt[name] = 0

    def op(self, eng, fn, waits=(), inc=None, inc_k=1):
        sem = self.sem
        w = [(sem[s], v) for (s, v) in waits if v and v > 0]
        if inc is not None:
            self.cnt[inc] += inc_k
            tgt = (sem[inc], inc_k)
        else:
            tgt = None

        def f(eng_obj, w=w, tgt=tgt, fn=fn):
            for s, v in w:
                eng_obj.wait_ge(s, v)
            ins = fn(eng_obj)
            if tgt is not None:
                if ins is None:
                    ins = eng_obj.nop()
                ins.then_inc(*tgt)
        self.ops[eng].append(f)
        return self.cnt[inc] if inc is not None else None

    def dma(self, queue, out_ap, in_ap, waits=(), sem=None):
        if sem is None:
            s = {"sync": "qS", "scalar": "qA", "gpsimd": "qG"}[queue]
            waits = list(waits) + [(s, self.cnt[s])]
        else:
            s = sem
            self.new_sem(s)
        return self.op(queue, lambda e: e.dma_start(out=out_ap, in_=in_ap),
                       waits=waits, inc=s, inc_k=16)

    def barrier(self):
        snap = [(s, v) for s, v in self.cnt.items() if v > 0]
        for e in ENGINES:
            self.op(e, lambda eng: None, waits=snap)


def build_program():
    nc = bass.Bass("TRN2", target_bir_lowering=False)
    dt_ = nc.dram_tensor

    x0T = dt_("x0T", [D, T], F32, kind="ExternalInput").ap()
    wfT = dt_("wfT", [D, D], F32, kind="ExternalInput").ap()
    wqa = dt_("wqa", [D, CQ], F32, kind="ExternalInput").ap()
    wqbn = dt_("wqbn", [CQ, H * 128], F32, kind="ExternalInput").ap()
    wqbr = dt_("wqbr", [CQ, H * DR], F32, kind="ExternalInput").ap()
    wkva = dt_("wkva", [D, CKV], F32, kind="ExternalInput").ap()
    wkvb = dt_("wkvb", [CKV, H * 256], F32, kind="ExternalInput").ap()
    wkr = dt_("wkr", [D, DR], F32, kind="ExternalInput").ap()
    wo = dt_("wo", [H * 128, D], F32, kind="ExternalInput").ap()
    gamma_t = dt_("gamma_t", [128, NCH], F32, kind="ExternalInput").ap()
    beta_t = dt_("beta_t", [128, NCH], F32, kind="ExternalInput").ap()
    lnw_t = dt_("lnw_t", [128, NCH], F32, kind="ExternalInput").ap()
    lnb_t = dt_("lnb_t", [128, NCH], F32, kind="ExternalInput").ap()
    qnw_t = dt_("qnw_t", [128, NCQ], F32, kind="ExternalInput").ap()
    kvnw_t = dt_("kvnw_t", [128, NKV], F32, kind="ExternalInput").ap()
    ones_cb = dt_("ones_cb", [128, 1], BF16, kind="ExternalInput").ap()
    ones_rf = dt_("ones_rf", [1, 128], F32, kind="ExternalInput").ap()
    ident_f = dt_("ident_f", [128, 128], F32, kind="ExternalInput").ap()
    mskin = dt_("mskin", [128, 32 * G], BF16, kind="ExternalInput").ap()
    cs_q = dt_("cs_q", [64, OWN], BF16, kind="ExternalInput").ap()
    sc_q = dt_("sc_q", [64, OWN], BF16, kind="ExternalInput").ap()
    cs_k = dt_("cs_k", [64, T], BF16, kind="ExternalInput").ap()
    sc_k = dt_("sc_k", [64, T], BF16, kind="ExternalInput").ap()
    offs = dt_("offs", [1, 8], I32, kind="ExternalInput").ap()

    yT = dt_("yT", [128, NCH * OWN], F32, kind="ExternalOutput").ap()

    d_sw = dt_("d_sw", [128, NCH * D], BF16).ap()
    d_x1 = dt_("d_x1", [128, NCH * T], F32).ap()
    d_x1n = dt_("d_x1n", [128, NCH * T], BF16).ap()
    d_cqn = dt_("d_cqn", [128, NCQ * OWN], BF16).ap()
    d_qn = dt_("d_qn", [128, H * OWN], BF16).ap()
    d_qr = dt_("d_qr", [64, H * OWN], BF16).ap()
    d_ot = dt_("d_ot", [128, H * OWN], BF16).ap()
    d_x2 = dt_("d_x2", [128, NCH * OWN], F32).ap()
    d_zn2 = dt_("d_zn2", [128, NCH * OWN], BF16).ap()

    def v3(ap2d, nch):
        return ap2d.rearrange("p (c t) -> p c t", c=nch)

    with ExitStack() as X:
        E = X.enter_context
        p = Prog(nc, X)
        ps = [E(nc.psum_tensor(f"ps{i}", [128, 512], F32)) for i in range(8)]

        c_onesb = E(nc.sbuf_tensor("c_onesb", [128, 1], BF16))
        c_onesrf = E(nc.sbuf_tensor("c_onesrf", [1, 128], F32))
        c_gam = E(nc.sbuf_tensor("c_gam", [128, NCH], F32))
        c_bet = E(nc.sbuf_tensor("c_bet", [128, NCH], F32))
        c_lnw = E(nc.sbuf_tensor("c_lnw", [128, NCH], F32))
        c_lnb = E(nc.sbuf_tensor("c_lnb", [128, NCH], F32))
        c_qnw = E(nc.sbuf_tensor("c_qnw", [128, NCQ], F32))
        c_kvnw = E(nc.sbuf_tensor("c_kvnw", [128, NKV], F32))
        c_Gam = E(nc.sbuf_tensor("c_Gam", [128, NCH], F32))
        c_B2 = E(nc.sbuf_tensor("c_B2", [128, NCH], F32))
        c_scol = E(nc.sbuf_tensor("c_scol", [128, 1], F32))
        c_csq = E(nc.sbuf_tensor("c_csq", [64, OWN], BF16))
        c_scq = E(nc.sbuf_tensor("c_scq", [64, OWN], BF16))
        c_csk = E(nc.sbuf_tensor("c_csk", [64, T], BF16))
        c_sck = E(nc.sbuf_tensor("c_sck", [64, T], BF16))
        c_offs = E(nc.sbuf_tensor("c_offs", [1, 8], I32))
        c_idb = E(nc.sbuf_tensor("c_idb", [128, 128], BF16))
        ckvT = E(nc.sbuf_tensor("ckvT", [128, NKV * T], BF16))
        krotT = E(nc.sbuf_tensor("krotT", [64, T], BF16))
        qmax = E(nc.sbuf_tensor("qmax", [1, 128], F32))
        kstat = E(nc.sbuf_tensor("kstat", [1, 64], F32))
        rows = E(nc.sbuf_tensor("rows", [1, 5 * G], F32))

        def rowchain(pe_end, bank_sum, bank_sq, double, denom, extra_w=()):
            p.op("vector",
                 lambda e: e.tensor_scalar(rows[0:1, 0:G],
                                           bank_sum[0:1, 0:G],
                                           1.0 / denom, None, ALU.mult),
                 waits=[("pe", pe_end)] + list(extra_w))
            v0 = p.op("vector",
                      lambda e: e.tensor_scalar(rows[0:1, G:2 * G],
                                                bank_sq[0:1, 0:G],
                                                1.0 / denom, None, ALU.mult),
                      inc="ve")
            p.op("vector",
                 lambda e: e.tensor_mul(rows[0:1, 4 * G:5 * G],
                                        rows[0:1, 0:G], rows[0:1, 0:G]))
            p.op("vector",
                 lambda e: e.tensor_sub(rows[0:1, G:2 * G],
                                        rows[0:1, G:2 * G],
                                        rows[0:1, 4 * G:5 * G]))
            if double:
                v1 = p.op("vector",
                          lambda e: e.tensor_scalar(
                              rows[0:1, 2 * G:3 * G], rows[0:1, G:2 * G],
                              1.0 + EPS2, EPS1 * EPS2, ALU.mult, ALU.add),
                          inc="ve")
            else:
                v1 = p.op("vector",
                          lambda e: e.tensor_scalar(
                              rows[0:1, 2 * G:3 * G], rows[0:1, G:2 * G],
                              1.0, EPS1, ALU.mult, ALU.add), inc="ve")
            p.op("scalar",
                 lambda e: e.activation(rows[0:1, 2 * G:3 * G],
                                        rows[0:1, 2 * G:3 * G], AF.Ln),
                 waits=[("ve", v1)])
            a1c = p.op("scalar",
                       lambda e: e.activation(rows[0:1, 2 * G:3 * G],
                                              rows[0:1, 2 * G:3 * G],
                                              AF.Exp, scale=-0.5), inc="ac")
            vmr = p.op("vector",
                       lambda e: e.tensor_mul(rows[0:1, 3 * G:4 * G],
                                              rows[0:1, 0:G],
                                              rows[0:1, 2 * G:3 * G]),
                       waits=[("ac", a1c)], inc="ve")
            return a1c, vmr, v0

        def chunk_stats(src_f32, castbuf, sqbuf, bank_sum, bank_sq, first_w):
            pe_sum = {}
            pe_sq = {-1: 0}
            ac_cast = {}
            ve_sq = {-1: 0}
            for c in range(NCH):
                ac_cast[c] = p.op(
                    "scalar",
                    lambda e, c=c: e.copy(castbuf[:, :],
                                          src_f32[:, c * G:(c + 1) * G]),
                    waits=(list(first_w) if c == 0 else
                           [("pe", pe_sum[c - 1]), ("ve", ve_sq[c - 1])]),
                    inc="ac")
                ve_sq[c] = p.op(
                    "vector",
                    lambda e: e.tensor_mul(sqbuf[:, :], castbuf[:, :],
                                           castbuf[:, :]),
                    waits=[("ac", ac_cast[c]), ("pe", pe_sq[c - 1])],
                    inc="ve")
                pe_sum[c] = p.op(
                    "tensor",
                    lambda e, c=c: e.matmul(bank_sum[0:1, 0:G],
                                            c_onesb[:, 0:1], castbuf[:, :],
                                            start=(c == 0),
                                            stop=(c == NCH - 1)),
                    waits=[("ac", ac_cast[c])], inc="pe")
                pe_sq[c] = p.op(
                    "tensor",
                    lambda e, c=c: e.matmul(bank_sq[0:1, 0:G],
                                            c_onesb[:, 0:1], sqbuf[:, :],
                                            start=(c == 0),
                                            stop=(c == NCH - 1)),
                    waits=[("ve", ve_sq[c])], inc="pe")
            return pe_sq[NCH - 1], ac_cast[NCH - 1], ve_sq[NCH - 1]

        def chunk_stats_b(src_b16, sqbuf, bank_sum, bank_sq, first_w,
                          nch=NCH, rows64=False):
            onesc = c_onesb[0:64, 0:1] if rows64 else c_onesb[:, 0:1]
            pe_sq = {-1: 0}
            ve_sq = {}
            for c in range(nch):
                p.op("tensor",
                     lambda e, c=c: e.matmul(
                         bank_sum[0:1, 0:G], onesc,
                         src_b16[:, c * G:(c + 1) * G],
                         start=(c == 0), stop=(c == nch - 1)),
                     waits=(list(first_w) if c == 0 else ()))
            for c in range(nch):
                ve_sq[c] = p.op(
                    "vector",
                    lambda e, c=c: e.tensor_mul(
                        sqbuf[:, :], src_b16[:, c * G:(c + 1) * G],
                        src_b16[:, c * G:(c + 1) * G]),
                    waits=([("pe", pe_sq[c - 1])] if c > 0 else
                           list(first_w)), inc="ve")
                pe_sq[c] = p.op(
                    "tensor",
                    lambda e, c=c: e.matmul(
                        bank_sq[0:1, 0:G], onesc, sqbuf[:, :],
                        start=(c == 0), stop=(c == nch - 1)),
                    waits=[("ve", ve_sq[c])], inc="pe")
            return pe_sq[nch - 1], ve_sq[nch - 1]

        p.dma("sync", c_onesb[:, :], ones_cb)
        p.dma("sync", c_onesrf[:, :], ones_rf)
        p.dma("sync", c_gam[:, :], gamma_t)
        p.dma("sync", c_bet[:, :], beta_t)
        p.dma("sync", c_lnw[:, :], lnw_t)
        p.dma("sync", c_lnb[:, :], lnb_t)
        p.dma("sync", c_qnw[:, :], qnw_t)
        p.dma("sync", c_kvnw[:, :], kvnw_t)
        p.dma("sync", c_csq[:, :], cs_q)
        p.dma("sync", c_scq[:, :], sc_q)
        p.dma("sync", c_csk[:, :], cs_k)
        p.dma("sync", c_sck[:, :], sc_k)
        cl = p.dma("sync", c_offs[:, :], offs)
        idbl = p.dma("gpsimd", c_idb[:, :], ident_f)

        with nc.sbuf_tensor("a0_st", [128, 2 * D], F32) as stage, \
             nc.sbuf_tensor("a0_sw", [128, 2 * D], BF16) as swt, \
             nc.sbuf_tensor("a0_wm", [128, NCH], F32) as wmax, \
             nc.sbuf_tensor("a0_id", [128, 128], F32) as identt, \
             nc.sbuf_tensor("a0_wc", [128, 1], F32) as wcol:
            idl = p.dma("sync", identt[:, :], ident_f)
            st_c, ac_c, ve_c, sp_c = {}, {}, {}, {}
            for k in range(NCH):
                b = k % 2
                st_c[k] = p.dma("sync", stage[:, b * D:(b + 1) * D],
                                wfT[k * 128:(k + 1) * 128, :],
                                waits=[("ac", ac_c.get(k - 2, 0)),
                                       ("ve", ve_c.get(k - 2, 0))])
                ac_c[k] = p.op("scalar",
                               lambda e, b=b: e.sign(
                                   swt[:, b * D:(b + 1) * D],
                                   stage[:, b * D:(b + 1) * D]),
                               waits=[("qS", st_c[k]),
                                      (f"qSW{k % 2}", sp_c.get(k - 2, 0))],
                               inc="ac")
                ve_c[k] = p.op("vector",
                               lambda e, b=b, k=k: e.reduce_max(
                                   wmax[:, k:k + 1],
                                   stage[:, b * D:(b + 1) * D],
                                   axis=AX.X, apply_absolute_value=True),
                               waits=[("qS", st_c[k])], inc="ve")
                sp_c[k] = p.dma("gpsimd", v3(d_sw, NCH)[:, k, :],
                                swt[:, b * D:(b + 1) * D],
                                waits=[("ve", ve_c[k]), ("ac", ac_c[k])],
                                sem=f"qSW{k % 2}")
            vm = p.op("vector",
                      lambda e: e.reduce_max(wcol[:, 0:1], wmax[:, :],
                                             axis=AX.X),
                      waits=[("ve", ve_c[NCH - 1])], inc="ve")
            tm = p.op("tensor",
                      lambda e: e.transpose(ps[0][0:1, 0:128], wcol[:, 0:1],
                                            identt[:, :]),
                      waits=[("ve", vm), ("qS", idl)], inc="pe")
            vs = p.op("vector",
                      lambda e: e.reduce_max(rows[0:1, 0:1],
                                             ps[0][0:1, 0:128], axis=AX.X),
                      waits=[("pe", tm)], inc="ve")
            tb = p.op("tensor",
                      lambda e: e.matmul(ps[1][:, 0:1], c_onesrf[0:1, :],
                                         rows[0:1, 0:1], start=True,
                                         stop=True),
                      waits=[("ve", vs)], inc="pe")
            vs = p.op("vector",
                      lambda e: e.tensor_copy(c_scol[:, 0:1], ps[1][:, 0:1]),
                      waits=[("pe", tb)], inc="ve")
            p.op("vector",
                 lambda e: e.tensor_scalar(c_Gam[:, :], c_gam[:, :],
                                           c_scol[:, 0:1], 0.5, ALU.mult,
                                           ALU.mult),
                 waits=[("ve", vs)])
            a0_ve = p.op("vector",
                         lambda e: e.tensor_scalar(c_B2[:, :], c_bet[:, :],
                                                   0.5, None, ALU.mult),
                         inc="ve")
            a0_sw0 = p.cnt["qSW0"]
            a0_sw1 = p.cnt["qSW1"]
            a0_pe = p.cnt["pe"]

        p.barrier()

        NG1 = T // G
        with nc.sbuf_tensor("a1_swb", [128, 6 * 512], BF16) as swb, \
             nc.sbuf_tensor("a1_x0f", [128, NCH * G], F32) as x0f, \
             nc.sbuf_tensor("a1_x0b", [128, 2 * NCH * G], BF16) as x0b, \
             nc.sbuf_tensor("a1_sq", [128, G], BF16) as sqb, \
             nc.sbuf_tensor("a1_cb", [128, G], BF16) as castb, \
             nc.sbuf_tensor("a1_zn", [128, 2 * NCH * G], BF16) as zn, \
             nc.sbuf_tensor("a1_zn1", [128, NCH * G], BF16) as zn1, \
             nc.sbuf_tensor("a1_bc", [128, 4 * G], F32) as bc, \
             nc.sbuf_tensor("a1_bcB", [128, 2 * G], BF16) as bcB:
            x0v = x0T.rearrange("(c p) t -> p c t", p=128)
            WZ = NCH * G
            swv = v3(d_sw, NCH)

            ld_f, ld_b, ld_sw = {}, {}, {}
            ve_zn, ve_zn1 = {}, {}

            def emit_ldb(g, pe_stm, stm_ve):
                if g >= NG1:
                    return
                xb_ = x0b[:, (g % 2) * NCH * G:(g % 2 + 1) * NCH * G]
                ld_b[g] = p.dma(
                    "gpsimd", v3(xb_, NCH), x0v[:, :, g * G:(g + 1) * G],
                    waits=[("pe", pe_stm.get(g - 2, 0)),
                           ("ve", stm_ve.get(g - 2, 0))], sem=f"qB{g % 2}")
            ve_epi = {}
            ve_v0m, ve_v0t = {}, {}
            ac_tail = {}
            pe_stm, pe_stt, pe_bcm, pe_bct, pe_pass, pe_swc = {}, {}, {}, {}, {}, {}
            sp_x1, sp_zn1 = {}, {}
            vbm, vbt = {}, {}
            stm_ve, stt_ac, stt_ve = {}, {}, {}

            for g in range(NG1):
                ld_f[g] = p.dma(
                    "sync", v3(x0f[:, :], NCH), x0v[:, :, g * G:(g + 1) * G],
                    waits=[("qX", sp_x1.get(g - 1, 0)),
                           ("ac", ac_tail.get(g - 1, 0)),
                           ("ve", ve_zn1.get(g - 1, 0))])
                if g == 0:
                    emit_ldb(0, pe_stm, stm_ve)
                    emit_ldb(1, pe_stm, stm_ve)
                xb = x0b[:, (g % 2) * NCH * G:(g % 2 + 1) * NCH * G]
                pe_stm[g], stm_ve[g] = chunk_stats_b(
                    xb, sqb, ps[4], ps[5],
                    [(f"qB{g % 2}", ld_b[g]), ("ve", ve_v0t.get(g - 1, 0)),
                     ("pe", pe_stt.get(g - 1, 0)),
                     ("pe", a0_pe if g == 0 else 0)])
                emit_ldb(g + 2, pe_stm, stm_ve)
                ac_rm, ve_mrm, ve_v0m[g] = rowchain(
                    pe_stm[g], ps[4], ps[5], True, D,
                    extra_w=[("pe", pe_bct.get(g - 1, 0))])
                p.op("tensor",
                     lambda e: e.matmul(ps[6][:, 0:G], c_onesrf[0:1, :],
                                        rows[0:1, 2 * G:3 * G],
                                        start=True, stop=True),
                     waits=[("ac", ac_rm), ("ve", ve_mrm),
                            ("ve", vbt.get(g - 1, 0))])
                pe_bcm[g] = p.op(
                    "tensor",
                    lambda e: e.matmul(ps[7][:, 0:G], c_onesrf[0:1, :],
                                       rows[0:1, 3 * G:4 * G],
                                       start=True, stop=True), inc="pe")
                p.op("vector",
                     lambda e: e.tensor_copy(bc[:, 0:G], ps[6][:, 0:G]),
                     waits=[("pe", pe_bcm[g])])
                vbm[g] = p.op("vector",
                              lambda e: e.tensor_copy(bc[:, G:2 * G],
                                                      ps[7][:, 0:G]),
                              inc="ve")
                p.op("vector",
                     lambda e: e.tensor_copy(bcB[:, 0:G], bc[:, G:2 * G]))
                zb = (g % 2) * WZ
                p.op("vector",
                     lambda e, zb=zb: e.tensor_tensor(
                         v3(zn[:, zb:zb + WZ], NCH), v3(x0f[:, :], NCH),
                         bc[:, 0:G].unsqueeze(1).broadcast_to((128, NCH, G)),
                         ALU.mult),
                     waits=[("qS", ld_f[g]),
                            ("pe", pe_pass.get((g - 2, LASTP), 0))])
                p.op("vector",
                     lambda e, zb=zb: e.tensor_tensor(
                         v3(zn[:, zb:zb + WZ], NCH),
                         v3(zn[:, zb:zb + WZ], NCH),
                         bcB[:, 0:G].unsqueeze(1).broadcast_to((128, NCH, G)),
                         ALU.subtract))
                for c in range(NCH):
                    ve_zn[g] = p.op(
                        "vector",
                        lambda e, c=c, zb=zb: e.tensor_scalar(
                            zn[:, zb + c * G: zb + (c + 1) * G],
                            zn[:, zb + c * G: zb + (c + 1) * G],
                            c_lnw[:, c:c + 1], c_lnb[:, c:c + 1],
                            ALU.mult, ALU.add),
                        inc="ve" if c == NCH - 1 else None)
                zb = (g % 2) * WZ
                for pi, olist in enumerate(OP3):
                    o0 = olist[0]
                    for k in range(NCH):
                        pk = _prev_chunk_user(k) if k < 6 else None
                        if pk is not None:
                            wv = pe_swc.get((g, pi - 1, pk),
                                            pe_swc.get((g - 1, LASTP, pk), 0))
                        else:
                            wv = pe_swc[(g, pi, k - 6)]
                        ld_sw[(g, pi, k)] = p.dma(
                            "scalar",
                            swb[:, (k % 6) * 512:
                                (k % 6) * 512 + len(olist) * 128],
                            swv[:, k, o0 * 128:(o0 + len(olist)) * 128],
                            waits=[("pe", wv)] +
                                  ([("qSW0", a0_sw0), ("qSW1", a0_sw1)]
                                   if (g == 0 and pi == 0) else []),
                            sem=f"qW{k % 6}")
                        for j, o in enumerate(olist):
                            w = []
                            if j == 0:
                                w.append((f"qW{k % 6}", ld_sw[(g, pi, k)]))
                                if k == 0:
                                    w.append(("ve", ve_zn[g]))
                                    w.append(("ve", ve_epi.get(
                                        (g, pi - 1),
                                        ve_epi.get((g - 1, LASTP), 0))))
                            last = (j == len(olist) - 1)
                            cnt = p.op(
                                "tensor",
                                lambda e, k=k, j=j, o=o, o0=o0, zb=zb:
                                e.matmul(
                                    ps[o - o0][:, 0:G],
                                    swb[:, (k % 6) * 512 + j * 128:
                                        (k % 6) * 512 + (j + 1) * 128],
                                    zn[:, zb + k * G: zb + (k + 1) * G],
                                    start=(k == 0), stop=(k == NCH - 1)),
                                waits=w, inc="pe" if last else None)
                            if last:
                                pe_swc[(g, pi, k)] = cnt
                    pe_pass[(g, pi)] = p.cnt["pe"]
                    for o in olist:
                        p.op("vector",
                             lambda e, o=o, o0=o0: e.scalar_tensor_tensor(
                                 x0f[:, o * G:(o + 1) * G], ps[o - o0][:, 0:G],
                                 c_Gam[:, o:o + 1],
                                 x0f[:, o * G:(o + 1) * G], ALU.mult, ALU.add),
                             waits=([("pe", pe_pass[(g, pi)])]
                                    if o == olist[0] else ()))
                    for o in olist:
                        ve_epi[(g, pi)] = p.op(
                            "vector",
                            lambda e, o=o: e.tensor_scalar(
                                x0f[:, o * G:(o + 1) * G],
                                x0f[:, o * G:(o + 1) * G],
                                c_B2[:, o:o + 1], None, ALU.add),
                            inc="ve" if o == olist[-1] else None)
                sp_x1[g] = p.dma(
                    "sync", v3(d_x1, NCH)[:, :, g * G:(g + 1) * G],
                    v3(x0f[:, :], NCH), waits=[("ve", ve_epi[(g, LASTP)])],
                    sem="qX")
                pe_stt[g], ac_tail[g], _vet = chunk_stats(
                    x0f, castb, sqb, ps[4], ps[5],
                    [("ve", ve_epi[(g, LASTP)]), ("ve", ve_v0m[g])])
                ac_rt, ve_mrt, ve_v0t[g] = rowchain(
                    pe_stt[g], ps[4], ps[5], False, D,
                    extra_w=[("pe", pe_bcm[g])])
                p.op("tensor",
                     lambda e: e.matmul(ps[6][:, 0:G], c_onesrf[0:1, :],
                                        rows[0:1, 2 * G:3 * G],
                                        start=True, stop=True),
                     waits=[("ac", ac_rt), ("ve", ve_mrt),
                            ("ve", vbm[g])])
                pe_bct[g] = p.op(
                    "tensor",
                    lambda e: e.matmul(ps[7][:, 0:G], c_onesrf[0:1, :],
                                       rows[0:1, 3 * G:4 * G],
                                       start=True, stop=True), inc="pe")
                p.op("vector",
                     lambda e: e.tensor_copy(bc[:, 2 * G:3 * G],
                                             ps[6][:, 0:G]),
                     waits=[("pe", pe_bct[g])])
                vbt[g] = p.op("vector",
                              lambda e: e.tensor_copy(bc[:, 3 * G:4 * G],
                                                      ps[7][:, 0:G]),
                              inc="ve")
                p.op("vector",
                     lambda e: e.tensor_copy(bcB[:, G:2 * G],
                                             bc[:, 3 * G:4 * G]))
                p.op("vector",
                     lambda e: e.tensor_tensor(
                         v3(zn1[:, :], NCH), v3(x0f[:, :], NCH),
                         bc[:, 2 * G:3 * G].unsqueeze(1).broadcast_to(
                             (128, NCH, G)), ALU.mult),
                     waits=[("qX", sp_x1[g]), ("qZ", sp_zn1.get(g - 1, 0))])
                ve_zn1[g] = p.op(
                    "vector",
                    lambda e: e.tensor_tensor(
                        v3(zn1[:, :], NCH), v3(zn1[:, :], NCH),
                        bcB[:, G:2 * G].unsqueeze(1).broadcast_to(
                            (128, NCH, G)), ALU.subtract), inc="ve")
                sp_zn1[g] = p.dma(
                    "sync", v3(d_x1n, NCH)[:, :, g * G:(g + 1) * G],
                    v3(zn1[:, :], NCH), waits=[("ve", ve_zn1[g])], sem="qZ")
            a1_x1n = sp_zn1[NG1 - 1]
            a1_x1 = sp_x1[NG1 - 1]

        p.barrier()

        with nc.sbuf_tensor("a2_w", [128, NCH * CQ], BF16) as wqab, \
             nc.sbuf_tensor("a2_x", [128, 2 * NCH * G], BF16) as x1nb, \
             nc.sbuf_tensor("a2_cq", [128, NCQ * G], BF16) as cqb, \
             nc.sbuf_tensor("a2_sq", [128, G], BF16) as cqs, \
             nc.sbuf_tensor("a2_bc", [128, G], F32) as rbc:
            ld_w = p.dma("gpsimd", v3(wqab[:, :], NCH),
                         wqa.rearrange("(c p) t -> p c t", p=128),
                         waits=[("ve", a0_ve)])
            ld_x, pe_cq, ve_ap, sp_cq = {}, {}, {}, {}
            ve_cqd, v0d = {}, {}
            for g in range(NSLOT):
                b = g % 2

                def ldfn(e, g=g, b=b):
                    r = e.value_load(c_offs[0:1, g:g + 1])
                    return e.dma_start(
                        out=v3(x1nb[:, b * NCH * G:(b + 1) * NCH * G], NCH),
                        in_=v3(d_x1n, NCH)[:, :, bass.ts(r, G)])
                ld_x[g] = p.op("sync", ldfn,
                               waits=[("qS", p.cnt["qS"]), ("qZ", a1_x1n),
                                      ("pe", pe_cq.get(g - 2, 0))],
                               inc="qS", inc_k=16)
                for cc in range(NCQ):
                    for k in range(NCH):
                        w = []
                        if k == 0:
                            w = [("qS", ld_x[g]), ("qG", ld_w),
                                 ("ve", ve_ap.get((g - 1, cc), 0)),
                                 ("ve", ve_cqd.get((g, cc - 2), 0))]
                        pe_cq[g] = p.op(
                            "tensor",
                            lambda e, cc=cc, k=k, b=b: e.matmul(
                                ps[cc % 2][:, 0:G],
                                wqab[:, k * CQ + cc * 128:
                                     k * CQ + (cc + 1) * 128],
                                x1nb[:, b * NCH * G + k * G:
                                     b * NCH * G + (k + 1) * G],
                                start=(k == 0), stop=(k == NCH - 1)),
                            waits=w, inc="pe" if k == NCH - 1 else None)
                    ve_cq = p.op("vector",
                                 lambda e, cc=cc: e.tensor_copy(
                                     cqb[:, cc * G:(cc + 1) * G],
                                     ps[cc % 2][:, 0:G]),
                                 waits=[("pe", pe_cq[g])] +
                                       ([("qC", sp_cq.get(g - 1, 0))]
                                        if cc == 0 else []), inc="ve")
                    ve_cqd[(g, cc)] = ve_cq
                    vs = p.op("vector",
                              lambda e, cc=cc: e.tensor_mul(
                                  cqs[:, :], cqb[:, cc * G:(cc + 1) * G],
                                  cqb[:, cc * G:(cc + 1) * G]),
                              waits=[("ve", ve_cq)], inc="ve")
                    pe_ssq = p.op(
                        "tensor",
                        lambda e, cc=cc: e.matmul(
                            ps[2][0:1, 0:G], c_onesb[:, 0:1], cqs[:, :],
                            start=(cc == 0), stop=(cc == NCQ - 1)),
                        waits=[("ve", vs)] +
                              ([("ve", v0d.get(g - 1, 0))] if cc == 0
                               else []),
                        inc="pe" if cc == NCQ - 1 else None)
                v0 = p.op("vector",
                          lambda e: e.tensor_scalar(
                              rows[0:1, 0:G], ps[2][0:1, 0:G], 1.0 / CQ, EPSR,
                              ALU.mult, ALU.add),
                          waits=[("pe", pe_ssq)], inc="ve")
                v0d[g] = v0
                p.op("scalar", lambda e: e.activation(rows[0:1, 0:G],
                                                      rows[0:1, 0:G], AF.Ln),
                     waits=[("ve", v0)])
                a1c = p.op("scalar",
                           lambda e: e.activation(rows[0:1, 0:G],
                                                  rows[0:1, 0:G], AF.Exp,
                                                  scale=-0.5), inc="ac")
                pb = p.op("tensor",
                          lambda e: e.matmul(ps[3][:, 0:G], c_onesrf[0:1, :],
                                             rows[0:1, 0:G], start=True,
                                             stop=True),
                          waits=[("ac", a1c)], inc="pe")
                vb = p.op("vector",
                          lambda e: e.tensor_copy(rbc[:, :], ps[3][:, 0:G]),
                          waits=[("pe", pb)], inc="ve")
                for cc in range(NCQ):
                    ve_ap[(g, cc)] = p.op(
                        "vector",
                        lambda e, cc=cc: e.tensor_tensor(
                            cqb[:, cc * G:(cc + 1) * G],
                            cqb[:, cc * G:(cc + 1) * G], rbc[:, :], ALU.mult),
                        waits=([("ve", vb), ("qC", sp_cq.get(g - 1, 0))]
                               if cc == 0 else ()), inc="ve")
                sp_cq[g] = p.dma("gpsimd",
                                 v3(d_cqn, NCQ)[:, :, g * G:(g + 1) * G],
                                 v3(cqb[:, :], NCQ),
                                 waits=[("ve", ve_ap[(g, NCQ - 1)])],
                                 sem="qC")
            a2a_done = sp_cq[NSLOT - 1]

        p.barrier()

        with nc.sbuf_tensor("b1_w", [128, NCQ * (H * 128)], BF16) as wqnb, \
             nc.sbuf_tensor("b1_c", [128, 2 * NCQ * G], BF16) as cqnb, \
             nc.sbuf_tensor("b1_q", [128, 2 * H * G], BF16) as qng, \
             nc.sbuf_tensor("b1_s", [128, 2 * G], BF16) as qsq:
            ld_w = p.dma("gpsimd", v3(wqnb[:, :], NCQ),
                         wqbn.rearrange("(c p) t -> p c t", p=128))
            ve_wf = 0
            for cc in range(NCQ):
                ve_wf = p.op("vector",
                             lambda e, cc=cc: e.tensor_scalar(
                                 wqnb[:, cc * (H * 128):(cc + 1) * (H * 128)],
                                 wqnb[:, cc * (H * 128):(cc + 1) * (H * 128)],
                                 c_qnw[:, cc:cc + 1], None, ALU.mult),
                             waits=[("qG", ld_w)] if cc == 0 else (),
                             inc="ve")
            ld_c, pe_q, ac_ev, sp_qn, ve_red = {}, {}, {}, {}, {}
            pe_qn = {}
            for g in range(NSLOT):
                b = g % 2
                qh1 = qng[:, b * H * G:(b + 1) * H * G]
                ld_c[g] = p.dma(
                    "sync", v3(cqnb[:, b * NCQ * G:(b + 1) * NCQ * G], NCQ),
                    v3(d_cqn, NCQ)[:, :, g * G:(g + 1) * G],
                    waits=[("qC", a2a_done), ("pe", pe_q.get(g - 2, 0))])
                for h in range(H):
                    for cc in range(NCQ):
                        w = []
                        if cc == 0:
                            w = [("qS", ld_c[g]), ("ve", ve_wf),
                                 ("ac", ac_ev.get((g, h - 2), 0))]
                        pe_q[g] = p.op(
                            "tensor",
                            lambda e, h=h, cc=cc, b=b: e.matmul(
                                ps[h % 2][:, 0:G],
                                wqnb[:, cc * (H * 128) + h * 128:
                                     cc * (H * 128) + (h + 1) * 128],
                                cqnb[:, b * NCQ * G + cc * G:
                                     b * NCQ * G + (cc + 1) * G],
                                start=(cc == 0), stop=(cc == NCQ - 1)),
                            waits=w, inc="pe" if cc == NCQ - 1 else None)
                    ac_ev[(g, h)] = p.op(
                        "scalar",
                        lambda e, h=h, qh1=qh1: e.activation(
                            qh1[:, h * G:(h + 1) * G], ps[h % 2][:, 0:G],
                            AF.Copy, scale=SCALE),
                        waits=[("pe", pe_q[g]),
                               ("qN", sp_qn.get((g - 2, h), 0)),
                               ("ve", ve_red.get((g, h - 1), 0))], inc="ac")
                    hb1 = (h % 2) * G
                    vs = p.op("vector",
                              lambda e, h=h, qh1=qh1, hb1=hb1: e.tensor_mul(
                                  qsq[:, hb1:hb1 + G],
                                  qh1[:, h * G:(h + 1) * G],
                                  qh1[:, h * G:(h + 1) * G]),
                              waits=[("ac", ac_ev[(g, h)]),
                                     ("pe", pe_qn.get((g, h - 2), 0))],
                              inc="ve")
                    pe_qn[(g, h)] = p.op(
                        "tensor",
                        lambda e, hb1=hb1: e.matmul(
                            ps[2][0:1, 0:G], c_onesb[:, 0:1],
                            qsq[:, hb1:hb1 + G], start=True, stop=True),
                        waits=[("ve", vs)], inc="pe")
                    ve_red[(g, h)] = p.op(
                        "vector",
                        lambda e, h=h, g=g: e.reduce_max(
                            qmax[0:1, h * 8 + g * 2: h * 8 + g * 2 + 1],
                            ps[2][0:1, 0:G], axis=AX.X),
                        waits=[("pe", pe_qn[(g, h)])], inc="ve")
                    sp_qn[(g, h)] = p.dma(
                        "gpsimd", v3(d_qn, H)[:, h, g * G:(g + 1) * G],
                        qh1[:, h * G:(h + 1) * G],
                        waits=[("ve", ve_red[(g, h)]),
                               ("qN", p.cnt["qN"] if "qN" in p.cnt else 0)],
                        sem="qN")
            a2b1_done = sp_qn[(NSLOT - 1, H - 1)]

        p.barrier()

        with nc.sbuf_tensor("b2_w", [128, NCQ * (H * DR)], BF16) as wqrb, \
             nc.sbuf_tensor("b2_c", [128, 2 * NCQ * G], BF16) as cqnb2, \
             nc.sbuf_tensor("b2_q", [64, 2 * H * G], BF16) as qrg, \
             nc.sbuf_tensor("b2_q2", [64, H * G], BF16) as qr2, \
             nc.sbuf_tensor("b2_p1", [64, 4 * G], BF16) as rp1, \
             nc.sbuf_tensor("b2_p2", [64, 4 * G], BF16) as rp2, \
             nc.sbuf_tensor("b2_s", [64, 2 * G], BF16) as qsq2:
            ld_w = p.dma("gpsimd", v3(wqrb[:, :], NCQ),
                         wqbr.rearrange("(c p) t -> p c t", p=128))
            ve_wf = 0
            for cc in range(NCQ):
                ve_wf = p.op("vector",
                             lambda e, cc=cc: e.tensor_scalar(
                                 wqrb[:, cc * (H * DR):(cc + 1) * (H * DR)],
                                 wqrb[:, cc * (H * DR):(cc + 1) * (H * DR)],
                                 c_qnw[:, cc:cc + 1], None, ALU.mult),
                             waits=[("qG", ld_w)] if cc == 0 else (),
                             inc="ve")
            ld_c, pe_q, ac_ev, sp_qr, ve_rope = {}, {}, {}, {}, {}
            pe_qr = {}
            vr_done = {}
            for g in range(NSLOT):
                b = g % 2
                qh = qrg[:, b * H * G:(b + 1) * H * G]
                ld_c[g] = p.dma(
                    "sync", v3(cqnb2[:, b * NCQ * G:(b + 1) * NCQ * G], NCQ),
                    v3(d_cqn, NCQ)[:, :, g * G:(g + 1) * G],
                    waits=[("pe", pe_q.get(g - 2, 0))])
                for h in range(H):
                    for cc in range(NCQ):
                        w = []
                        if cc == 0:
                            w = [("qS", ld_c[g]), ("ve", ve_wf),
                                 ("ac", ac_ev.get((g, h - 2), 0)),
                                 (f"qR{g % 2}",
                                  sp_qr.get(g - 2, 0) if h == 0 else 0)]
                        pe_q[g] = p.op(
                            "tensor",
                            lambda e, h=h, cc=cc, b=b: e.matmul(
                                ps[h % 2][0:64, 0:G],
                                wqrb[:, cc * (H * DR) + h * DR:
                                     cc * (H * DR) + (h + 1) * DR],
                                cqnb2[:, b * NCQ * G + cc * G:
                                     b * NCQ * G + (cc + 1) * G],
                                start=(cc == 0), stop=(cc == NCQ - 1)),
                            waits=w, inc="pe" if cc == NCQ - 1 else None)
                    ac_ev[(g, h)] = p.op(
                        "scalar",
                        lambda e, h=h, qh=qh: e.activation(
                            qh[:, h * G:(h + 1) * G], ps[h % 2][0:64, 0:G],
                            AF.Copy, scale=SCALE),
                        waits=[("pe", pe_q[g]),
                               ("ve", vr_done.get(g - 2, 0) if h < 2 else 0)],
                        inc="ac")
                sw1 = p.dma("sync", qr2[0:32, :], qh[32:64, :],
                            waits=[("ac", ac_ev[(g, H - 1)]),
                                   ("ve", ve_rope.get(g - 1, 0))])
                sw2 = p.dma("sync", qr2[32:64, :], qh[0:32, :])
                lastr = 0
                for hq in range(4):
                    sl = slice(hq * 4 * G, (hq + 1) * 4 * G)
                    csl = c_csq[:, g * G:(g + 1) * G].unsqueeze(1)\
                        .broadcast_to((64, 4, G))
                    ssl = c_scq[:, g * G:(g + 1) * G].unsqueeze(1)\
                        .broadcast_to((64, 4, G))
                    qv4 = qh[:, sl].rearrange("p (h t) -> p h t", h=4)
                    qw4 = qr2[:, sl].rearrange("p (h t) -> p h t", h=4)
                    p.op("vector",
                         lambda e, qv4=qv4, csl=csl: e.tensor_tensor(
                             rp1[:, :].rearrange("p (h t) -> p h t", h=4),
                             qv4, csl, ALU.mult),
                         waits=[("qS", sw2)] if hq == 0 else ())
                    p.op("vector",
                         lambda e, qw4=qw4, ssl=ssl: e.tensor_tensor(
                             rp2[:, :].rearrange("p (h t) -> p h t", h=4),
                             qw4, ssl, ALU.mult))
                    lastr = p.op("vector",
                                 lambda e, sl=sl, qh=qh: e.tensor_add(
                                     qh[:, sl], rp1[:, :], rp2[:, :]),
                                 inc="ve")
                ve_rope[g] = lastr
                for h in range(H):
                    hb2 = (h % 2) * G
                    vs = p.op("vector",
                              lambda e, h=h, qh=qh, hb2=hb2: e.tensor_mul(
                                  qsq2[:, hb2:hb2 + G],
                                  qh[:, h * G:(h + 1) * G],
                                  qh[:, h * G:(h + 1) * G]),
                              waits=([("ve", ve_rope[g])] if h <= 1 else
                                     [("pe", pe_qr.get((g, h - 2), 0))]),
                              inc="ve")
                    pe_qr[(g, h)] = p.op(
                        "tensor",
                        lambda e, hb2=hb2: e.matmul(
                            ps[2][0:1, 0:G],
                            c_onesb[0:64, 0:1], qsq2[:, hb2:hb2 + G],
                            start=True, stop=True),
                        waits=[("ve", vs)], inc="pe")
                    p.op("vector",
                         lambda e, h=h, g=g: e.reduce_max(
                             qmax[0:1, h * 8 + g * 2 + 1:h * 8 + g * 2 + 2],
                             ps[2][0:1, 0:G], axis=AX.X),
                         waits=[("pe", pe_qr[(g, h)])], inc="ve")
                vr_done[g] = p.cnt["ve"]
                sp_qr[g] = p.dma(
                    "gpsimd",
                    d_qr.rearrange("p (h t) -> p h t", h=H)[
                        :, :, g * G:(g + 1) * G],
                    qh.rearrange("p (h t) -> p h t", h=H),
                    waits=[("ve", vr_done[g])], sem=f"qR{g % 2}")
            a2b2_done = [("qR0", p.cnt["qR0"]), ("qR1", p.cnt["qR1"])]

        p.barrier()

        NG3 = T // G
        with nc.sbuf_tensor("a3_wa", [128, NCH * CKV], BF16) as wkvab, \
             nc.sbuf_tensor("a3_wr", [128, NCH * DR], BF16) as wkrb, \
             nc.sbuf_tensor("a3_x", [128, 2 * NCH * G], BF16) as x1nb3, \
             nc.sbuf_tensor("a3_cv", [128, NKV * G], BF16) as cvb, \
             nc.sbuf_tensor("a3_s", [128, G], BF16) as cvs, \
             nc.sbuf_tensor("a3_kr", [64, 2 * G], BF16) as krg, \
             nc.sbuf_tensor("a3_k2", [64, G], BF16) as kr2g, \
             nc.sbuf_tensor("a3_rp", [64, 2 * G], BF16) as krp, \
             nc.sbuf_tensor("a3_bc", [128, G], F32) as rbc3:
            ld_wa = p.dma("gpsimd", v3(wkvab[:, :], NCH),
                          wkva.rearrange("(c p) t -> p c t", p=128))
            ld_wr = p.dma("gpsimd", v3(wkrb[:, :], NCH),
                          wkr.rearrange("(c p) t -> p c t", p=128))
            ld_x, pe_cv, ve_ap, pe_kr, ac_kr = {}, {}, {}, {}, {}
            swk2 = {}
            ve_cvd, v0d3 = {}, {}
            kr2parts = []
            for g in range(NG3):
                b = g % 2
                ld_x[g] = p.dma(
                    "sync", v3(x1nb3[:, b * NCH * G:(b + 1) * NCH * G], NCH),
                    v3(d_x1n, NCH)[:, :, g * G:(g + 1) * G],
                    waits=[("qZ", a1_x1n), ("pe", pe_kr.get(g - 2, 0))],
                    sem=f"qL3{g % 2}")
                for cc in range(NKV):
                    for k in range(NCH):
                        w = []
                        if k == 0:
                            w = [(f"qL3{g % 2}", ld_x[g]), ("qG", ld_wa),
                                 ("ve", ve_ap.get((g - 1, cc), 0)),
                                 ("ve", ve_cvd.get((g, cc - 2), 0))]
                        pe_cv[g] = p.op(
                            "tensor",
                            lambda e, cc=cc, k=k, b=b: e.matmul(
                                ps[cc % 2][:, 0:G],
                                wkvab[:, k * CKV + cc * 128:
                                      k * CKV + (cc + 1) * 128],
                                x1nb3[:, b * NCH * G + k * G:
                                     b * NCH * G + (k + 1) * G],
                                start=(k == 0), stop=(k == NCH - 1)),
                            waits=w, inc="pe" if k == NCH - 1 else None)
                    ve_cv = p.op("vector",
                                 lambda e, cc=cc: e.tensor_copy(
                                     cvb[:, cc * G:(cc + 1) * G],
                                     ps[cc % 2][:, 0:G]),
                                 waits=[("pe", pe_cv[g])], inc="ve")
                    ve_cvd[(g, cc)] = ve_cv
                    vs = p.op("vector",
                              lambda e, cc=cc: e.tensor_mul(
                                  cvs[:, :], cvb[:, cc * G:(cc + 1) * G],
                                  cvb[:, cc * G:(cc + 1) * G]),
                              waits=[("ve", ve_cv)], inc="ve")
                    pe_ss = p.op(
                        "tensor",
                        lambda e, cc=cc: e.matmul(
                            ps[2][0:1, 0:G], c_onesb[:, 0:1], cvs[:, :],
                            start=(cc == 0), stop=(cc == NKV - 1)),
                        waits=[("ve", vs)] +
                              ([("ve", kr2parts[g - 1] if g > 0 else 0)]
                               if cc == 0 else []),
                        inc="pe" if cc == NKV - 1 else None)
                v0 = p.op("vector",
                          lambda e: e.tensor_scalar(
                              rows[0:1, 0:G], ps[2][0:1, 0:G], 1.0 / CKV,
                              EPSR, ALU.mult, ALU.add),
                          waits=[("pe", pe_ss)], inc="ve")
                v0d3[g] = v0
                p.op("scalar", lambda e: e.activation(rows[0:1, 0:G],
                                                      rows[0:1, 0:G], AF.Ln),
                     waits=[("ve", v0)])
                a1c = p.op("scalar",
                           lambda e: e.activation(rows[0:1, 0:G],
                                                  rows[0:1, 0:G], AF.Exp,
                                                  scale=-0.5), inc="ac")
                pb = p.op("tensor",
                          lambda e: e.matmul(ps[3][:, 0:G], c_onesrf[0:1, :],
                                             rows[0:1, 0:G], start=True,
                                             stop=True),
                          waits=[("ac", a1c)], inc="pe")
                vb = p.op("vector",
                          lambda e: e.tensor_copy(rbc3[:, :], ps[3][:, 0:G]),
                          waits=[("pe", pb)], inc="ve")
                for cc in range(NKV):
                    ve_ap[(g, cc)] = p.op(
                        "vector",
                        lambda e, cc=cc, g=g: e.tensor_tensor(
                            ckvT[:, cc * T + g * G: cc * T + (g + 1) * G],
                            cvb[:, cc * G:(cc + 1) * G], rbc3[:, :], ALU.mult),
                        waits=[("ve", vb)] if cc == 0 else (), inc="ve")
                for k in range(NCH):
                    pe_kr[g] = p.op(
                        "tensor",
                        lambda e, k=k, b=b: e.matmul(
                            ps[3][0:64, 0:G], wkrb[:, k * DR:(k + 1) * DR],
                            x1nb3[:, b * NCH * G + k * G:
                                 b * NCH * G + (k + 1) * G],
                            start=(k == 0), stop=(k == NCH - 1)),
                        waits=([("qG", ld_wr), ("ve", vb),
                                (f"qL3{g % 2}", ld_x[g])] if k == 0 else ()),
                        inc="pe" if k == NCH - 1 else None)
                kb3 = (g % 2) * G
                ac_kr[g] = p.op("scalar",
                                lambda e, kb3=kb3: e.copy(
                                    krg[:, kb3:kb3 + G], ps[3][0:64, 0:G]),
                                waits=[("pe", pe_kr[g]),
                                       ("qS", swk2.get(g - 2, 0))], inc="ac")
                vkr = p.op("vector",
                           lambda e, kb3=kb3: e.tensor_mul(
                               krp[:, 0:G], krg[:, kb3:kb3 + G],
                               krg[:, kb3:kb3 + G]),
                           waits=[("ac", ac_kr[g])], inc="ve")
                pk2 = p.op("tensor",
                           lambda e: e.matmul(ps[2][0:1, 0:G],
                                              c_onesb[0:64, 0:1],
                                              krp[:, 0:G], start=True,
                                              stop=True),
                           waits=[("ve", vkr)], inc="pe")
                kr2parts.append(p.op(
                    "vector",
                    lambda e, g=g: e.reduce_max(kstat[0:1, g:g + 1],
                                                ps[2][0:1, 0:G], axis=AX.X),
                    waits=[("pe", pk2)], inc="ve"))
                swk1 = p.dma("sync", kr2g[0:32, :],
                             krg[32:64, kb3:kb3 + G],
                             waits=[("ac", ac_kr[g]),
                                    ("ve", kr2parts[-1])])
                swk2[g] = p.dma("sync", kr2g[32:64, :],
                                krg[0:32, kb3:kb3 + G])
                csl = c_csk[:, g * G:(g + 1) * G]
                ssl = c_sck[:, g * G:(g + 1) * G]
                p.op("vector",
                     lambda e, csl=csl, kb3=kb3: e.tensor_tensor(
                         krp[:, 0:G], krg[:, kb3:kb3 + G], csl,
                         ALU.mult))
                p.op("vector",
                     lambda e, ssl=ssl: e.tensor_tensor(krp[:, G:2 * G],
                                                        kr2g[:, :], ssl,
                                                        ALU.mult),
                     waits=[("qS", swk2[g])])
                p.op("vector",
                     lambda e, g=g: e.tensor_add(
                         krotT[:, g * G:(g + 1) * G],
                         krp[:, 0:G], krp[:, G:2 * G]), inc="ve")
            p.op("vector",
                 lambda e: e.reduce_max(kstat[0:1, 8:9], kstat[0:1, 0:NG3],
                                        axis=AX.X),
                 waits=[("ve", kr2parts[-1])], inc="ve")
            a3_ve = p.cnt["ve"]
            a3_pe = p.cnt["pe"]

        p.barrier()

        with nc.sbuf_tensor("b_wb", [128, NKV * (H * 256)], BF16) as wkvbb, \
             nc.sbuf_tensor("b_msk", [128, 32 * G], BF16) as msks, \
             nc.sbuf_tensor("b_kp", [128, 2 * T], BF16) as kpass, \
             nc.sbuf_tensor("b_v", [128, 2 * T], BF16) as vkm, \
             nc.sbuf_tensor("b_qn", [128, 2 * OWN], BF16) as qnh, \
             nc.sbuf_tensor("b_qr", [64, 2 * OWN], BF16) as qrh, \
             nc.sbuf_tensor("b_pt", [128, 2 * G], BF16) as ptb, \
             nc.sbuf_tensor("b_s", [128, G], BF16) as ksq, \
             nc.sbuf_tensor("b_vs", [128, 2 * G], BF16) as vsb, \
             nc.sbuf_tensor("b_ot", [128, 2 * G], BF16) as otb, \
             nc.sbuf_tensor("b_rl", [128, 2 * G], F32) as rlb, \
             nc.sbuf_tensor("b_mb", [128, 4], F32) as mbt:
            ld_wb = p.dma("gpsimd", v3(wkvbb[:, :], NKV),
                          wkvb.rearrange("(c p) t -> p c t", p=128),
                          waits=[("ve", a3_ve)])
            ld_msk = p.dma("sync", msks[:, :], mskin)
            ve_wf = 0
            for cc in range(NKV):
                ve_wf = p.op("vector",
                             lambda e, cc=cc: e.tensor_scalar(
                                 wkvbb[:, cc * (H * 256):
                                       (cc + 1) * (H * 256)],
                                 wkvbb[:, cc * (H * 256):
                                       (cc + 1) * (H * 256)],
                                 c_kvnw[:, cc:cc + 1], None, ALU.mult),
                             waits=[("qG", ld_wb)] if cc == 0 else (),
                             inc="ve")
            ld_qn, ld_qr, pe_hd, ve_hd, sp_ot = {}, {}, {}, {}, {}
            for h in range(H):
                hb = h % 2
                ld_qn[h] = p.dma("sync", qnh[:, hb * OWN:(hb + 1) * OWN],
                                 v3(d_qn, H)[:, h, :],
                                 waits=[("qN", a2b1_done),
                                        ("pe", pe_hd.get(h - 2, 0))],
                                 sem=f"qQN{h % 2}")
                ld_qr[h] = p.dma("sync", qrh[:, hb * OWN:(hb + 1) * OWN],
                                 d_qr.rearrange("p (h t) -> p h t", h=H)[
                                     :, h, :],
                                 waits=list(a2b2_done) +
                                       [("pe", pe_hd.get(h - 2, 0))],
                                 sem=f"qQR{h % 2}")
                pe_kg, ac_kg, kp2r = {}, {}, {}
                pe_k2 = {}
                for kg in range(T // G):
                    for cc in range(NKV):
                        pe_kg[kg] = p.op(
                            "tensor",
                            lambda e, kg=kg, cc=cc, h=h: e.matmul(
                                ps[6 + kg % 2][:, 0:G],
                                wkvbb[:, cc * (H * 256) + h * 256:
                                      cc * (H * 256) + h * 256 + 128],
                                ckvT[:, cc * T + kg * G:
                                     cc * T + (kg + 1) * G],
                                start=(cc == 0), stop=(cc == NKV - 1)),
                            waits=([("ve", ve_wf),
                                    ("ac", ac_kg.get(kg - 2, 0)),
                                    ("ve", ve_hd.get(h - 1, 0)),
                                    ("pe", a3_pe if (h == 0 and kg == 0)
                                     else 0)]
                                   if cc == 0 else ()),
                            inc="pe" if cc == NKV - 1 else None)
                    ac_kg[kg] = p.op(
                        "scalar",
                        lambda e, kg=kg, hb=hb: e.copy(
                            kpass[:, hb * T + kg * G: hb * T + (kg + 1) * G],
                            ps[6 + kg % 2][:, 0:G]),
                        waits=[("pe", pe_kg[kg])], inc="ac")
                    vs = p.op("vector",
                              lambda e, kg=kg, hb=hb: e.tensor_mul(
                                  ksq[:, :],
                                  kpass[:, hb * T + kg * G:
                                        hb * T + (kg + 1) * G],
                                  kpass[:, hb * T + kg * G:
                                        hb * T + (kg + 1) * G]),
                              waits=[("ac", ac_kg[kg]),
                                     ("pe", pe_k2.get(kg - 1, 0))], inc="ve")
                    pe_k2[kg] = p.op(
                        "tensor",
                        lambda e: e.matmul(ps[2][0:1, 0:G], c_onesb[:, 0:1],
                                           ksq[:, :], start=True, stop=True),
                        waits=[("ve", vs)], inc="pe")
                    kp2r[kg] = p.op(
                        "vector",
                        lambda e, kg=kg: e.reduce_max(
                            kstat[0:1, 16 + kg:17 + kg], ps[2][0:1, 0:G],
                            axis=AX.X),
                        waits=[("pe", pe_k2[kg])], inc="ve")
                vk1 = p.op("vector",
                           lambda e: e.reduce_max(kstat[0:1, 24:25],
                                                  kstat[0:1, 16:24],
                                                  axis=AX.X),
                           waits=[("ve", kp2r[T // G - 1])], inc="ve")
                vk2 = p.op("vector",
                           lambda e: e.tensor_add(kstat[0:1, 25:26],
                                                  kstat[0:1, 24:25],
                                                  kstat[0:1, 8:9]),
                           waits=[("ve", vk1)], inc="ve")
                p.op("scalar",
                     lambda e: e.activation(kstat[0:1, 25:26],
                                            kstat[0:1, 25:26], AF.Ln),
                     waits=[("ve", vk2)])
                ak = p.op("scalar",
                          lambda e: e.activation(kstat[0:1, 25:26],
                                                 kstat[0:1, 25:26], AF.Exp,
                                                 scale=0.5), inc="ac")
                mb_done = {}
                for sl_ in range(NSLOT):
                    v1 = p.op("vector",
                              lambda e, h=h, sl_=sl_: e.tensor_add(
                                  kstat[0:1, 32 + sl_:33 + sl_],
                                  qmax[0:1, h * 8 + sl_ * 2:
                                       h * 8 + sl_ * 2 + 1],
                                  qmax[0:1, h * 8 + sl_ * 2 + 1:
                                       h * 8 + sl_ * 2 + 2]),
                              waits=[("ac", ak)], inc="ve")
                    p.op("scalar",
                         lambda e, sl_=sl_: e.activation(
                             kstat[0:1, 32 + sl_:33 + sl_],
                             kstat[0:1, 32 + sl_:33 + sl_], AF.Ln),
                         waits=[("ve", v1)])
                    a2c = p.op("scalar",
                               lambda e, sl_=sl_: e.activation(
                                   kstat[0:1, 32 + sl_:33 + sl_],
                                   kstat[0:1, 32 + sl_:33 + sl_], AF.Exp,
                                   scale=0.5), inc="ac")
                    v2 = p.op("vector",
                              lambda e, sl_=sl_: e.tensor_mul(
                                  kstat[0:1, 40 + sl_:41 + sl_],
                                  kstat[0:1, 32 + sl_:33 + sl_],
                                  kstat[0:1, 25:26]),
                              waits=[("ac", a2c)], inc="ve")
                    v2 = p.op("vector",
                              lambda e, sl_=sl_: e.tensor_scalar(
                                  kstat[0:1, 40 + sl_:41 + sl_],
                                  kstat[0:1, 40 + sl_:41 + sl_], -1.0, None,
                                  ALU.mult), inc="ve")
                    pb = p.op("tensor",
                              lambda e, sl_=sl_: e.matmul(
                                  ps[2][:, sl_:sl_ + 1], c_onesrf[0:1, :],
                                  kstat[0:1, 40 + sl_:41 + sl_],
                                  start=True, stop=True),
                              waits=[("ve", v2)], inc="pe")
                    mb_done[sl_] = p.op(
                        "vector",
                        lambda e, sl_=sl_: e.tensor_copy(
                            mbt[:, sl_:sl_ + 1], ps[2][:, sl_:sl_ + 1]),
                        waits=[("pe", pb)], inc="ve")
                ve_vg = {}
                pe_vg = {}
                for k4 in range(T // G):
                    for j in range(4):
                        kt = k4 * 4 + j
                        for cc in range(NKV):
                            pe_vg[k4] = p.op(
                                "tensor",
                                lambda e, kt=kt, j=j, cc=cc, h=h, k4=k4:
                                e.matmul(
                                    ps[6 + k4 % 2][:, j * 128:(j + 1) * 128],
                                    ckvT[:, cc * T + kt * 128:
                                         cc * T + (kt + 1) * 128],
                                    wkvbb[:, cc * (H * 256) + h * 256 + 128:
                                          cc * (H * 256) + (h + 1) * 256],
                                    start=(cc == 0 and j == 0),
                                    stop=(cc == NKV - 1 and j == 3)),
                                waits=([("ve", ve_vg.get(k4 - 2, 0)),
                                        ("ac", ac_kg[T // G - 1])]
                                       if (j == 0 and cc == 0) else ()),
                                inc="pe" if (cc == NKV - 1 and j == 3)
                                else None)
                    ve_vg[k4] = p.op(
                        "vector",
                        lambda e, k4=k4, hb=hb: e.tensor_copy(
                            vkm[:, hb * T + k4 * G: hb * T + (k4 + 1) * G],
                            ps[6 + k4 % 2][:, 0:G]),
                        waits=[("pe", pe_vg[k4])], inc="ve")
                ve_vg_done = ve_vg[T // G - 1]

                ve_ot = {}
                vb_prev = 0
                last_exp_prev = 0
                for sl_ in range(NSLOT):
                    nkt = NKT[sl_]
                    reg0 = nkt - 8
                    ac_exp, ve_msk, pe_lpv, pe_sc = {}, {}, {}, {}

                    def emit_lpv(i, sl_=sl_, nkt=nkt, ac_exp=None,
                                 ve_msk=None, pe_lpv=None, ve_ot=ve_ot, h=h,
                                 hb=hb):
                        w = [("ve", ve_msk[i])] if i in ve_msk else \
                            [("ac", ac_exp[i])]
                        if i == 0:
                            w.append(("ve", ve_ot.get(sl_ - 2, 0)))
                            if sl_ < 2:
                                w.append(("ve", ve_hd.get(h - 1, 0)))
                        p.op("tensor",
                             lambda e, i=i, sl_=sl_, nkt=nkt: e.matmul(
                                 ps[2 + sl_ % 2][0:1, 0:G], c_onesb[:, 0:1],
                                 ptb[:, (i % 2) * G:(i % 2 + 1) * G],
                                 start=(i == 0), stop=(i == nkt - 1)),
                             waits=w)
                        pe_lpv[i] = p.op(
                            "tensor",
                            lambda e, i=i, sl_=sl_, nkt=nkt, hb=hb: e.matmul(
                                ps[4 + sl_ % 2][:, 0:G],
                                vkm[:, hb * T + i * 128:
                                    hb * T + (i + 1) * 128],
                                ptb[:, (i % 2) * G:(i % 2 + 1) * G],
                                start=(i == 0), stop=(i == nkt - 1)),
                            inc="pe")

                    for i in range(nkt):
                        w = [("ac", ac_exp.get(i - 2, 0))]
                        if i == 0:
                            w += [("ac", ac_kg[T // G - 1]),
                                  (f"qQN{h % 2}", ld_qn[h]),
                                  (f"qQR{h % 2}", ld_qr[h]),
                                  ("ve", mb_done[NSLOT - 1]),
                                  ("ve", ve_vg_done),
                                  ("ac", last_exp_prev),
                                  ("ve", vb_prev)]
                        p.op("tensor",
                             lambda e, i=i, hb=hb, sl_=sl_: e.matmul(
                                 ps[i % 2][:, 0:G],
                                 kpass[:, hb * T + i * 128:
                                       hb * T + (i + 1) * 128],
                                 qnh[:, hb * OWN + sl_ * G:
                                     hb * OWN + (sl_ + 1) * G],
                                 start=True, stop=False),
                             waits=w)
                        pe_sc[i] = p.op(
                            "tensor",
                            lambda e, i=i, hb=hb, sl_=sl_: e.matmul(
                                ps[i % 2][:, 0:G],
                                krotT[:, i * 128:(i + 1) * 128],
                                qrh[:, hb * OWN + sl_ * G:
                                    hb * OWN + (sl_ + 1) * G],
                                start=False, stop=True),
                            inc="pe")
                        if i >= 1:
                            emit_lpv(i - 1, ac_exp=ac_exp, ve_msk=ve_msk,
                                     pe_lpv=pe_lpv)
                        ac_exp[i] = p.op(
                            "scalar",
                            lambda e, i=i, sl_=sl_: e.activation(
                                ptb[:, (i % 2) * G:(i % 2 + 1) * G],
                                ps[i % 2][:, 0:G], AF.Exp,
                                bias=mbt[:, sl_:sl_ + 1]),
                            waits=[("pe", pe_sc[i]),
                                   ("pe", pe_lpv.get(i - 2, 0))], inc="ac")
                        if i >= reg0:
                            j = i - reg0
                            ve_msk[i] = p.op(
                                "vector",
                                lambda e, i=i, j=j, sl_=sl_: e.tensor_mul(
                                    ptb[:, (i % 2) * G:(i % 2 + 1) * G],
                                    ptb[:, (i % 2) * G:(i % 2 + 1) * G],
                                    msks[:, (sl_ * 8 + j) * G:
                                         (sl_ * 8 + j + 1) * G]),
                                waits=[("ac", ac_exp[i]),
                                       ("qS", ld_msk if h == 0 else 0)],
                                inc="ve")
                    emit_lpv(nkt - 1, ac_exp=ac_exp, ve_msk=ve_msk,
                             pe_lpv=pe_lpv)
                    last_exp_prev = ac_exp[nkt - 1]
                    lpv_end = p.cnt["pe"]
                    vr = p.op("vector",
                              lambda e, sl_=sl_: e.reciprocal(
                                  rows[0:1, 0:G], ps[2 + sl_ % 2][0:1, 0:G]),
                              waits=[("pe", lpv_end)], inc="ve")
                    pb = p.op("tensor",
                              lambda e, sl_=sl_: e.matmul(
                                  ps[sl_ % 2][:, 0:G], c_onesrf[0:1, :],
                                  rows[0:1, 0:G], start=True, stop=True),
                              waits=[("ve", vr)], inc="pe")
                    vb_prev = p.op(
                        "vector",
                        lambda e, sl_=sl_: e.tensor_copy(
                            rlb[:, (sl_ % 2) * G:(sl_ % 2 + 1) * G],
                            ps[sl_ % 2][:, 0:G]),
                        waits=[("pe", pb)], inc="ve")
                    ve_ot[sl_] = p.op(
                        "vector",
                        lambda e, sl_=sl_: e.tensor_tensor(
                            otb[:, (sl_ % 2) * G:(sl_ % 2 + 1) * G],
                            ps[4 + sl_ % 2][:, 0:G],
                            rlb[:, (sl_ % 2) * G:(sl_ % 2 + 1) * G],
                            ALU.mult),
                        waits=[(f"qO{sl_}", sp_ot.get((h - 1, sl_), 0))],
                        inc="ve")
                    sp_ot[(h, sl_)] = p.dma(
                        "gpsimd", v3(d_ot, H)[:, h, sl_ * G:(sl_ + 1) * G],
                        otb[:, (sl_ % 2) * G:(sl_ % 2 + 1) * G],
                        waits=[("ve", ve_ot[sl_])], sem=f"qO{sl_}")
                pe_hd[h] = p.cnt["pe"]
                ve_hd[h] = p.cnt["ve"]
            b_done = [(f"qO{s}", p.cnt[f"qO{s}"]) for s in range(NSLOT)]

        p.barrier()

        with nc.sbuf_tensor("c1_wo", [128, 12 * 512], BF16) as wob, \
             nc.sbuf_tensor("c1_ot", [128, NCH * G], BF16) as otg, \
             nc.sbuf_tensor("c1_x1", [128, 2 * NCH * G], F32) as x1s, \
             nc.sbuf_tensor("c1_cb", [128, G], BF16) as castb2, \
             nc.sbuf_tensor("c1_sq", [128, G], BF16) as sqb2, \
             nc.sbuf_tensor("c1_zn", [128, NCH * G], BF16) as zn2, \
             nc.sbuf_tensor("c1_bc", [128, 2 * G], F32) as bc2, \
             nc.sbuf_tensor("c1_bB", [128, G], BF16) as bcB2:
            wo_v = wo.rearrange("(c p) t -> p c t", p=128)
            ld_ot, ld_x1, ld_wo = {}, {}, {}
            pe_woc, pe_pass2, ve_epi2, ve_zn2, ve_v02 = {}, {}, {}, {}, {}
            pe_st2, ac_c2, sp_x2, sp_zn2, pb2 = {}, {}, {}, {}, {}
            vb2 = {}
            for g in range(NSLOT):
                ld_ot[g] = p.dma(
                    "sync", v3(otg[:, :], NCH),
                    v3(d_ot, NCH)[:, :, g * G:(g + 1) * G],
                    waits=list(b_done) +
                          [("pe", pe_pass2.get((g - 1, LASTP), 0))])

                x1h = x1s[:, (g % 2) * NCH * G:(g % 2 + 1) * NCH * G]

                def ldx1(e, g=g, x1h=x1h):
                    r = e.value_load(c_offs[0:1, g:g + 1])
                    return e.dma_start(out=v3(x1h, NCH),
                                       in_=v3(d_x1, NCH)[:, :, bass.ts(r, G)])
                ld_x1[g] = p.op("sync", ldx1,
                                waits=[("qS", p.cnt["qS"]), ("qX", a1_x1),
                                       (f"qX2{g % 2}", sp_x2.get(g - 2, 0)),
                                       ("ac", ac_c2.get(g - 2, 0)),
                                       ("ve", ve_zn2.get(g - 2, 0))],
                                inc="qS", inc_k=16)
                for pi, olist in enumerate(OP3):
                    o0 = olist[0]
                    for hv in range(NCH):
                        if hv < 12:
                            pk = hv + 12 if hv + 12 <= 15 else hv
                            wv = pe_woc.get((g, pi - 1, pk),
                                            pe_woc.get((g - 1, LASTP, pk), 0))
                        else:
                            wv = pe_woc[(g, pi, hv - 12)]
                        ld_wo[(g, pi, hv)] = p.dma(
                            "gpsimd",
                            wob[:, (hv % 12) * 512:
                                (hv % 12) * 512 + len(olist) * 128],
                            wo_v[:, hv, o0 * 128:(o0 + len(olist)) * 128],
                            waits=[("pe", wv)], sem=f"qV{hv % 12}")
                        for j, o in enumerate(olist):
                            w = []
                            if j == 0:
                                w.append((f"qV{hv % 12}", ld_wo[(g, pi, hv)]))
                                if hv == 0:
                                    w.append(("qS", ld_ot[g]))
                                    w.append(("ve", ve_epi2.get(
                                        (g, pi - 1),
                                        ve_epi2.get((g - 1, LASTP), 0))))
                            last = (j == len(olist) - 1)
                            cnt = p.op(
                                "tensor",
                                lambda e, hv=hv, j=j, o=o, o0=o0: e.matmul(
                                    ps[o - o0][:, 0:G],
                                    wob[:, (hv % 12) * 512 + j * 128:
                                        (hv % 12) * 512 + (j + 1) * 128],
                                    otg[:, hv * G:(hv + 1) * G],
                                    start=(hv == 0), stop=(hv == NCH - 1)),
                                waits=w, inc="pe" if last else None)
                            if last:
                                pe_woc[(g, pi, hv)] = cnt
                    pe_pass2[(g, pi)] = p.cnt["pe"]
                    for o in olist:
                        ve_epi2[(g, pi)] = p.op(
                            "vector",
                            lambda e, o=o, o0=o0, x1h=x1h: e.scalar_tensor_tensor(
                                x1h[:, o * G:(o + 1) * G], ps[o - o0][:, 0:G],
                                1.0, x1h[:, o * G:(o + 1) * G],
                                ALU.mult, ALU.add),
                            waits=([("pe", pe_pass2[(g, pi)]),
                                    ("qS", ld_x1[g])] if o == olist[0]
                                   else ()),
                            inc="ve" if o == olist[-1] else None)
                sp_x2[g] = p.dma(
                    "scalar", v3(d_x2, NCH)[:, :, g * G:(g + 1) * G],
                    v3(x1h, NCH), waits=[("ve", ve_epi2[(g, LASTP)])],
                    sem=f"qX2{g % 2}")
                pe_st2[g], ac_c2[g], _ = chunk_stats(
                    x1h, castb2, sqb2, ps[4], ps[5],
                    [("ve", ve_epi2[(g, LASTP)]), ("ve", ve_v02.get(g - 1, 0))])
                ac_r2, ve_mr2, ve_v02[g] = rowchain(
                    pe_st2[g], ps[4], ps[5], True, D,
                    extra_w=[("pe", pb2.get(g - 1, 0))])
                p.op("tensor",
                     lambda e: e.matmul(ps[6][:, 0:G], c_onesrf[0:1, :],
                                        rows[0:1, 2 * G:3 * G], start=True,
                                        stop=True),
                     waits=[("ac", ac_r2), ("ve", ve_mr2),
                            ("ve", vb2.get(g - 1, 0))])
                pb2[g] = p.op("tensor",
                              lambda e: e.matmul(ps[7][:, 0:G],
                                                 c_onesrf[0:1, :],
                                                 rows[0:1, 3 * G:4 * G],
                                                 start=True, stop=True),
                              inc="pe")
                p.op("vector",
                     lambda e: e.tensor_copy(bc2[:, 0:G], ps[6][:, 0:G]),
                     waits=[("pe", pb2[g])])
                vb2[g] = p.op("vector",
                              lambda e: e.tensor_copy(bc2[:, G:2 * G],
                                                      ps[7][:, 0:G]),
                              inc="ve")
                p.op("vector",
                     lambda e: e.tensor_copy(bcB2[:, 0:G], bc2[:, G:2 * G]))
                p.op("vector",
                     lambda e, x1h=x1h: e.tensor_tensor(
                         v3(zn2[:, :], NCH), v3(x1h, NCH),
                         bc2[:, 0:G].unsqueeze(1).broadcast_to((128, NCH, G)),
                         ALU.mult),
                     waits=[(f"qX2{g % 2}", sp_x2[g]),
                            ("qZ2", sp_zn2.get(g - 1, 0))])
                p.op("vector",
                     lambda e: e.tensor_tensor(
                         v3(zn2[:, :], NCH), v3(zn2[:, :], NCH),
                         bcB2[:, 0:G].unsqueeze(1).broadcast_to(
                             (128, NCH, G)), ALU.subtract))
                for c in range(NCH):
                    ve_zn2[g] = p.op(
                        "vector",
                        lambda e, c=c: e.tensor_scalar(
                            zn2[:, c * G:(c + 1) * G],
                            zn2[:, c * G:(c + 1) * G],
                            c_lnw[:, c:c + 1], c_lnb[:, c:c + 1],
                            ALU.mult, ALU.add),
                        inc="ve" if c == NCH - 1 else None)
                sp_zn2[g] = p.dma(
                    "scalar", v3(d_zn2, NCH)[:, :, g * G:(g + 1) * G],
                    v3(zn2[:, :], NCH), waits=[("ve", ve_zn2[g])], sem="qZ2")
            c1_done = sp_zn2[NSLOT - 1]
            c1_x2 = [("qX20", p.cnt["qX20"]), ("qX21", p.cnt["qX21"])]

        p.barrier()

        with nc.sbuf_tensor("c2_sw", [128, 6 * 512], BF16) as swb2, \
             nc.sbuf_tensor("c2_zn", [128, 2 * NCH * G], BF16) as znc, \
             nc.sbuf_tensor("c2_x2", [128, 2 * NCH * G], F32) as x2s:
            swv = v3(d_sw, NCH)
            ld_zn, ld_x2, ld_sw2 = {}, {}, {}
            pe_swc2, pe_pass3, ve_epi3, sp_y = {}, {}, {}, {}
            for g in range(NSLOT):
                znh = znc[:, (g % 2) * NCH * G:(g % 2 + 1) * NCH * G]
                x2h = x2s[:, (g % 2) * NCH * G:(g % 2 + 1) * NCH * G]
                ld_zn[g] = p.dma(
                    "sync", v3(znh, NCH),
                    v3(d_zn2, NCH)[:, :, g * G:(g + 1) * G],
                    waits=[("qZ2", c1_done),
                           ("pe", pe_pass3.get((g - 2, LASTP), 0))])
                ld_x2[g] = p.dma(
                    "sync", v3(x2h, NCH),
                    v3(d_x2, NCH)[:, :, g * G:(g + 1) * G],
                    waits=list(c1_x2) +
                          [(f"qY{g % 2}", sp_y.get(g - 2, 0)),
                           ("ve", ve_epi3.get((g - 2, LASTP), 0))])
                for pi, olist in enumerate(OP3):
                    o0 = olist[0]
                    for k in range(NCH):
                        pk = _prev_chunk_user(k) if k < 6 else None
                        if pk is not None:
                            wv = pe_swc2.get((g, pi - 1, pk),
                                             pe_swc2.get((g - 1, LASTP, pk), 0))
                        else:
                            wv = pe_swc2[(g, pi, k - 6)]
                        ld_sw2[(g, pi, k)] = p.dma(
                            "scalar",
                            swb2[:, (k % 6) * 512:
                                 (k % 6) * 512 + len(olist) * 128],
                            swv[:, k, o0 * 128:(o0 + len(olist)) * 128],
                            waits=[("pe", wv)], sem=f"qW{k % 6}")
                        for j, o in enumerate(olist):
                            w = []
                            if j == 0:
                                w.append((f"qW{k % 6}", ld_sw2[(g, pi, k)]))
                                if k == 0:
                                    w.append(("qS", ld_zn[g]))
                                    w.append(("ve", ve_epi3.get(
                                        (g, pi - 1),
                                        ve_epi3.get((g - 1, LASTP), 0))))
                            last = (j == len(olist) - 1)
                            cnt = p.op(
                                "tensor",
                                lambda e, k=k, j=j, o=o, o0=o0, znh=znh:
                                e.matmul(
                                    ps[o - o0][:, 0:G],
                                    swb2[:, (k % 6) * 512 + j * 128:
                                         (k % 6) * 512 + (j + 1) * 128],
                                    znh[:, k * G:(k + 1) * G],
                                    start=(k == 0), stop=(k == NCH - 1)),
                                waits=w, inc="pe" if last else None)
                            if last:
                                pe_swc2[(g, pi, k)] = cnt
                    pe_pass3[(g, pi)] = p.cnt["pe"]
                    for o in olist:
                        p.op("vector",
                             lambda e, o=o, o0=o0, x2h=x2h:
                             e.scalar_tensor_tensor(
                                 x2h[:, o * G:(o + 1) * G],
                                 ps[o - o0][:, 0:G], c_Gam[:, o:o + 1],
                                 x2h[:, o * G:(o + 1) * G],
                                 ALU.mult, ALU.add),
                             waits=([("pe", pe_pass3[(g, pi)]),
                                     ("qS", ld_x2[g])] if o == olist[0]
                                    else ()))
                    for o in olist:
                        ve_epi3[(g, pi)] = p.op(
                            "vector",
                            lambda e, o=o, x2h=x2h: e.tensor_scalar(
                                x2h[:, o * G:(o + 1) * G],
                                x2h[:, o * G:(o + 1) * G],
                                c_B2[:, o:o + 1], None, ALU.add),
                            inc="ve" if o == olist[-1] else None)
                sp_y[g] = p.dma(
                    "gpsimd", v3(yT, NCH)[:, :, g * G:(g + 1) * G],
                    v3(x2h, NCH), waits=[("ve", ve_epi3[(g, LASTP)])],
                    sem=f"qY{g % 2}")
            p.op("gpsimd", lambda e: None,
                 waits=[("qY0", p.cnt["qY0"]), ("qY1", p.cnt["qY1"])])

        with nc.Block() as block:
            @block.sync
            def _(eng):
                for f in p.ops["sync"]:
                    f(eng)

            @block.gpsimd
            def _(eng):
                for f in p.ops["gpsimd"]:
                    f(eng)

            @block.scalar
            def _(eng):
                for f in p.ops["scalar"]:
                    f(eng)

            @block.vector
            def _(eng):
                for f in p.ops["vector"]:
                    f(eng)

            @block.tensor
            def _(eng):
                for f in p.ops["tensor"]:
                    f(eng)

    return nc


_PROG = []


def _get_program():
    if not _PROG:
        _PROG.append(build_program())
    return _PROG[0]


def _rope_tables(positions):
    inv = 1.0 / 10000.0 ** (np.arange(0, 64, 2, dtype=np.float64) / 64.0)
    fr = np.outer(positions.astype(np.float64), inv)
    c = np.cos(fr).astype(np.float32).T
    s = np.sin(fr).astype(np.float32).T
    cc = np.ascontiguousarray(np.concatenate([c, c], 0)).astype(BF)
    ssp = np.ascontiguousarray(np.concatenate([-s, s], 0)).astype(BF)
    return cc, ssp


def _masks(blocks):
    m = np.zeros((128, 32 * G), BF)
    for s in range(NSLOT):
        nkt = NKT[s]
        q0 = blocks[s] * G
        for j in range(8):
            kt = nkt - 8 + j
            kk = kt * 128 + np.arange(128)[:, None]
            qq = q0 + np.arange(G)[None, :]
            m[:, (s * 8 + j) * G:(s * 8 + j + 1) * G] = \
                (kk <= qq).astype(BF)
    return m


def kernel(L0_attn_hidden_states, L0_attn_Wq_a, L0_attn_q_norm_w,
           L0_attn_Wq_b, L0_attn_Wk_rope, L0_attn_Wkv_a, L0_attn_kv_norm_w,
           L0_attn_Wkv_b, L0_attn_Wo, L0_ffn_weight, L0_ffn_gamma,
           L0_ffn_beta, L0_ffn_ln_weight, L0_ffn_ln_bias):
    f32 = np.float32
    hs = np.asarray(L0_attn_hidden_states, f32)

    def pcol(v, nch):
        return np.ascontiguousarray(np.asarray(v, f32).reshape(nch, 128).T)

    wqb = np.asarray(L0_attn_Wq_b, f32).reshape(CQ, H, 192)
    shared = {
        "wfT": np.ascontiguousarray(np.asarray(L0_ffn_weight, f32).T),
        "wqa": np.asarray(L0_attn_Wq_a, f32),
        "wqbn": np.ascontiguousarray(wqb[:, :, :128].reshape(CQ, H * 128)),
        "wqbr": np.ascontiguousarray(wqb[:, :, 128:].reshape(CQ, H * DR)),
        "wkva": np.asarray(L0_attn_Wkv_a, f32),
        "wkvb": np.asarray(L0_attn_Wkv_b, f32),
        "wkr": np.asarray(L0_attn_Wk_rope, f32),
        "wo": np.asarray(L0_attn_Wo, f32),
        "gamma_t": pcol(L0_ffn_gamma, NCH),
        "beta_t": pcol(L0_ffn_beta, NCH),
        "lnw_t": pcol(L0_ffn_ln_weight, NCH),
        "qnw_t": pcol(L0_attn_q_norm_w, NCQ),
        "kvnw_t": pcol(L0_attn_kv_norm_w, NKV),
        "ones_cb": np.ones((128, 1), BF),
        "ones_rf": np.ones((1, 128), f32),
        "ident_f": np.eye(128, dtype=f32),
    }
    shared["lnb_t"] = pcol(L0_ffn_ln_bias, NCH)
    cs_k, sc_k = _rope_tables(np.arange(T))
    shared["cs_k"] = cs_k
    shared["sc_k"] = sc_k

    nc = _get_program()
    in_maps = []
    for core in range(8):
        bidx, half = core // 2, core % 2
        blocks = BLOCKS[half]
        pos = np.concatenate([np.arange(bb * G, (bb + 1) * G)
                              for bb in blocks])
        cs_q, sc_q = _rope_tables(pos)
        m = dict(shared)
        m["cs_q"] = cs_q
        m["sc_q"] = sc_q
        m["x0T"] = np.ascontiguousarray(hs[bidx].T)
        m["mskin"] = _masks(blocks)
        off = np.zeros((1, 8), np.int32)
        off[0, :NSLOT] = blocks
        m["offs"] = off
        in_maps.append(m)

    res = run_bass_kernel_spmd(nc, in_maps, core_ids=list(range(8)))

    out = np.empty((B, T, D), f32)
    for core in range(8):
        bidx, half = core // 2, core % 2
        blocks = BLOCKS[half]
        y = np.asarray(res.results[core]["yT"]).reshape(128, NCH, OWN)
        yt = y.transpose(2, 1, 0).reshape(OWN, D)
        for s, bb in enumerate(blocks):
            out[bidx, bb * G:(bb + 1) * G, :] = yt[s * G:(s + 1) * G, :]
    return out
